# version 20
# baseline (speedup 1.0000x reference)
import bisect
import numpy as np
import ml_dtypes
from contextlib import ExitStack
import concourse.bass as bass
import concourse.mybir as mybir
from concourse.bass_utils import run_bass_kernel_spmd
from concourse.aot_env import allow_ici_all_to_all

F32 = mybir.dt.float32
BF16 = mybir.dt.bfloat16
I32 = mybir.dt.int32
AF = mybir.ActivationFunctionType
ALU = mybir.AluOpType
AX = mybir.AxisListType

BF = ml_dtypes.bfloat16
NCORE = 8
TOK = 1024
DM = 4096
SEQ = 8192
NKT = DM // 128
EPS = 1e-5

ENGINES = ("T", "V", "S", "G", "Y")


class Prog:
    def __init__(self, name):
        self.nc = bass.Bass(target_bir_lowering=False, name=name)
        self.stream = {e: [] for e in ENGINES}
        self.cnt = {e: 0 for e in ENGINES}
        self.waited = {e: set() for e in ENGINES}
        self.dcnt = {}
        self.max_waited = {}
        self.ccnt = 0
        self.res = {}
        self.last_e = {e: 0 for e in ENGINES}
        self._ctx = ExitStack()
        self._sid = 0
        self._scope = None
        self._pending_bar = None
        self._esync = {}
        self._bar_tokens = None
        self.out_sems = set()

    def _r(self, name):
        return self.res.setdefault(name, {"w": None, "rs": []})

    def _gather(self, eng, reads, writes, sreads=()):
        waits = {}

        def add(tok, force=False):
            if tok is None:
                return
            if tok[0] == "E":
                _, e, seq = tok
                if e == eng and not force and eng == "T":
                    return
                self.waited[e].add(seq)
                key, val = "E" + e, seq
            elif tok[0] == "D":
                _, sem, k = tok
                val = self.dcnt[sem]
                self.max_waited[sem] = max(self.max_waited.get(sem, 0), val)
                key, tok = sem, ("D", sem, val)
            else:
                key, val = "CC", tok[1]
            old = waits.get(key)
            if old is None or tok[-1] > old[-1]:
                waits[key] = tok

        for r in reads:
            add(self._r(r)["w"])
        for r in sreads:
            add(self._r(r)["w"], force=True)
        for w in writes:
            rr = self._r(w)
            add(rr["w"])
            for t in rr["rs"]:
                add(t)
        return waits

    def _apply_barrier(self, eng, waits):
        if self._pending_bar and eng in self._pending_bar:
            self._pending_bar.discard(eng)
            self._esync[eng] = True
            for tok in self._bar_tokens:
                if tok[0] == "E":
                    self.waited[tok[1]].add(tok[2])
                    key = "E" + tok[1]
                elif tok[0] == "D":
                    key = tok[1]
                else:
                    key = "CC"
                old = waits.get(key)
                if old is None or tok[-1] > old[-1]:
                    waits[key] = tok

    def _commit(self, eng, waits, fn, mytok, reads, writes):
        for r in reads:
            self._r(r)["rs"].append(mytok)
        for w in writes:
            self.res[w] = {"w": mytok, "rs": []}
        self.stream[eng].append((list(waits.items()), fn, mytok))

    def op(self, eng, fn, reads=(), writes=(), sreads=()):
        waits = self._gather(eng, reads, writes, sreads)
        self._apply_barrier(eng, waits)
        self.cnt[eng] += 1
        self.last_e[eng] = self.cnt[eng]
        tok = ("E", eng, self.cnt[eng])
        self._commit(eng, waits, fn, tok, tuple(reads) + tuple(sreads), writes)
        return tok

    def dma(self, eng, sem, out, in_, reads=(), writes=(), out_sem=False, **kw):
        waits = self._gather(eng, reads, writes)
        self._apply_barrier(eng, waits)
        if self.max_waited.get(sem, 0) >= self.dcnt.get(sem, 0) and self.dcnt.get(sem, 0) > 0:
            if not waits and not self._esync.get(eng):
                raise AssertionError(f"dma on frozen sem {sem} without ordering deps")
        self.dcnt[sem] = self.dcnt.get(sem, 0) + 1
        tok = ("D", sem, self.dcnt[sem])
        if out_sem:
            self.out_sems.add(sem)
        self.cnt[eng] += 1
        self._commit(eng, waits,
                     (lambda e, out=out, in_=in_, kw=kw: e.dma_start(out=out, in_=in_, **kw)),
                     tok, reads, writes)
        return tok

    def cc(self, kind, ins, outs, reads=(), writes=(), op=None):
        eng = "G"
        waits = self._gather(eng, reads, writes)
        self._apply_barrier(eng, waits)
        if self.ccnt > 0:
            waits["CC"] = ("C", self.ccnt)
        self.ccnt += 1
        tok = ("C", self.ccnt)
        alu = op if op is not None else ALU.bypass

        def fn(e, kind=kind, alu=alu, ins=ins, outs=outs):
            with allow_ici_all_to_all():
                return e.collective_compute(kind, alu, replica_groups=[list(range(NCORE))],
                                            ins=ins, outs=outs)
        self.cnt[eng] += 1
        self._commit(eng, waits, fn, tok, reads, writes)
        return tok

    def sb(self, name, shape, dtype):
        ctx = self._scope if self._scope is not None else self._ctx
        return ctx.enter_context(self.nc.sbuf_tensor(f"s{self._sid}_{name}", shape, dtype))

    def psum(self, name, shape, dtype=F32):
        ctx = self._scope if self._scope is not None else self._ctx
        return ctx.enter_context(self.nc.psum_tensor(f"s{self._sid}_{name}", shape, dtype))

    def scope_open(self):
        assert self._scope is None
        self._sid += 1
        self._scope = ExitStack()

    def scope_close(self):
        self.barrier()
        self._scope.close()
        self._scope = None

    def barrier(self):
        toks = []
        for e in ENGINES:
            if self.last_e[e]:
                toks.append(("E", e, self.last_e[e]))
        for s, k in self.dcnt.items():
            toks.append(("D", s, k))
            self.max_waited[s] = k
        if self.ccnt:
            toks.append(("C", self.ccnt))
        self._bar_tokens = toks
        self._pending_bar = set(ENGINES)
        self._esync = {}

    def din(self, name, shape, dtype):
        return self.nc.dram_tensor(name, shape, dtype, kind="ExternalInput").ap()

    def dout(self, name, shape, dtype):
        return self.nc.dram_tensor(name, shape, dtype, kind="ExternalOutput").ap()

    def dint(self, name, shape, dtype, shared=False):
        if shared:
            return self.nc.dram_tensor(name, shape, dtype, addr_space="Shared").ap()
        return self.nc.dram_tensor(name, shape, dtype).ap()

    def finish(self):
        nc = self.nc
        sems = {}
        with self._ctx:
            for e in ENGINES:
                if self.waited[e]:
                    sems["E" + e] = self._ctx.enter_context(nc.semaphore("q" + e))
            for s in self.dcnt:
                sems[s] = self._ctx.enter_context(nc.semaphore("d_" + s))
            if self.ccnt:
                sems["CC"] = self._ctx.enter_context(nc.semaphore("d_cc"))
            block = self._ctx.enter_context(nc.Block())
            worder = {e: sorted(self.waited[e]) for e in ENGINES}

            def wval(tok):
                if tok[0] == "E":
                    return bisect.bisect_right(worder[tok[1]], tok[2])
                if tok[0] == "D":
                    return tok[2] * 16
                return tok[1]

            def make_run(key):
                def run(eng):
                    for waits, fn, tok in self.stream[key]:
                        for sem_key, wt in waits:
                            eng.wait_ge(sems[sem_key], wval(wt))
                        inst = fn(eng)
                        if tok[0] == "E":
                            if tok[2] in self.waited[key]:
                                inst.then_inc(sems["E" + key], 1)
                        elif tok[0] == "D":
                            inst.then_inc(sems[tok[1]], 16)
                        else:
                            inst.then_inc(sems["CC"], 1)
                    if key == "Y":
                        for s in sorted(self.out_sems):
                            eng.wait_ge(sems[s], self.dcnt[s] * 16)
                        if self.ccnt:
                            eng.wait_ge(sems["CC"], self.ccnt)
                return run

            block.tensor(make_run("T"))
            block.vector(make_run("V"))
            block.scalar(make_run("S"))
            block.gpsimd(make_run("G"))
            block.sync(make_run("Y"))
        return nc


def blocked_img(mat):
    K, M = mat.shape
    kt, mb = K // 128, M // 128
    return np.ascontiguousarray(
        mat.reshape(kt, 128, mb, 128).transpose(2, 1, 0, 3).reshape(mb, 128, kt * 128))


def small_img(mat):
    K, M = mat.shape
    kt = K // 128
    return np.ascontiguousarray(mat.reshape(kt, 128, M).transpose(1, 0, 2).reshape(128, kt * M))


def alloc_ln_root(p):
    c = {}
    c["xf"] = [p.sb(f"ln_xf{b}", [128, TOK], F32) for b in range(2)]
    c["sq"] = [p.sb(f"ln_sq{b}", [128, TOK], BF16) for b in range(2)]
    c["xc"] = [p.sb(f"ln_xc{b}", [128, TOK], BF16) for b in range(2)]
    c["t1"] = [p.sb(f"ln_t1{b}", [128, TOK], F32) for b in range(2)]
    c["m_row"] = p.sb("m_row", [1, TOK], F32)
    c["v_row"] = p.sb("v_row", [1, TOK], F32)
    c["rs_row"] = p.sb("rs_row", [1, TOK], F32)
    c["tmp_row"] = p.sb("tmp_row", [1, TOK], F32)
    c["ones_col"] = p.sb("ones_col", [128, 1], BF16)
    c["ones_row_f"] = p.sb("ones_row_f", [1, 128], F32)
    return c


def stage_ln(p, croot, want_f32=False):
    c = dict(croot)
    c["xln"] = p.sb("xln", [128, NKT * TOK], BF16)
    if want_f32:
        c["xlnf"] = [p.sb(f"xlnf{b}", [128, TOK], F32) for b in range(2)]
    c["pb"] = [p.psum(f"pb{i}", [128, 512], F32) for i in range(8)]
    return c


def alloc_ln(p, want_f32=False):
    c = stage_ln(p, alloc_ln_root(p), want_f32)
    return c


def emit_ln_consts(p, c, ones_col_d, ones_row_d):
    p.dma("Y", "cns", c["ones_col"][:, :], ones_col_d, writes=("ones_col",))
    p.dma("Y", "cns", c["ones_row_f"][:, :], ones_row_d, writes=("ones_row",))


def emit_ln(p, c, x_dram, want_f32=False, f32_sink=None):
    xf, sq, xc, pb = c["xf"], c["sq"], c["xc"], c["pb"]
    nb = len(xf)
    for k in range(NKT):
        b = k % nb
        p.dma("Y", f"lnx{b}", xf[b][:, :], x_dram[128 * k:128 * (k + 1), :],
              writes=(f"xf{b}",))
        p.op("V", lambda v, b=b: v.tensor_copy(xc[b][:, :], xf[b][:, :]),
             reads=(f"xf{b}",), writes=(f"xc{b}",))
        p.op("S", lambda s, b=b: s.activation(sq[b][:, :], xf[b][:, :], AF.Square),
             reads=(f"xf{b}",), writes=(f"sq{b}",))
        for h in range(2):
            p.op("T", lambda t, b=b, h=h, k=k: t.matmul(
                pb[h][0:1, :], c["ones_col"][:, 0:1],
                xc[b][:, 512 * h:512 * (h + 1)], start=(k == 0), stop=(k == NKT - 1)),
                reads=(f"xc{b}", "ones_col"), writes=(f"pb{h}",))
            p.op("T", lambda t, b=b, h=h, k=k: t.matmul(
                pb[2 + h][0:1, :], c["ones_col"][:, 0:1],
                sq[b][:, 512 * h:512 * (h + 1)], start=(k == 0), stop=(k == NKT - 1)),
                reads=(f"sq{b}", "ones_col"), writes=(f"pb{2 + h}",))
    mr, vr, rsr, tmp = c["m_row"], c["v_row"], c["rs_row"], c["tmp_row"]
    for h in range(2):
        p.op("V", lambda v, h=h: v.tensor_scalar(mr[:, 512 * h:512 * (h + 1)],
             pb[h][0:1, :], 1.0 / DM, None, ALU.mult),
             reads=(f"pb{h}",), writes=("m_row",))
        p.op("V", lambda v, h=h: v.tensor_scalar(vr[:, 512 * h:512 * (h + 1)],
             pb[2 + h][0:1, :], 1.0 / DM, None, ALU.mult),
             reads=(f"pb{2 + h}",), writes=("v_row",))
    p.op("V", lambda v: v.tensor_mul(tmp[:, :], mr[:, :], mr[:, :]),
         reads=("m_row",), writes=("tmp_row",))
    p.op("V", lambda v: v.tensor_sub(vr[:, :], vr[:, :], tmp[:, :]),
         reads=("v_row", "tmp_row"), writes=("v_row",))
    p.op("V", lambda v: v.tensor_scalar(vr[:, :], vr[:, :], float(EPS), None, ALU.add),
         reads=("v_row",), writes=("v_row",))
    p.op("S", lambda s: s.activation(vr[:, :], vr[:, :], AF.Sqrt),
         reads=("v_row",), writes=("v_row",))
    p.op("V", lambda v: v.reciprocal(rsr[:, :], vr[:, :]),
         reads=("v_row",), writes=("rs_row",))
    for h in range(2):
        p.op("T", lambda t, h=h: t.matmul(pb[4 + h][:, :], c["ones_row_f"][0:1, :],
             mr[:, 512 * h:512 * (h + 1)], start=True, stop=True),
             reads=("m_row", "ones_row"), writes=(f"pb{4 + h}",))
        p.op("T", lambda t, h=h: t.matmul(pb[6 + h][:, :], c["ones_row_f"][0:1, :],
             rsr[:, 512 * h:512 * (h + 1)], start=True, stop=True),
             reads=("rs_row", "ones_row"), writes=(f"pb{6 + h}",))
    t1 = c["t1"]
    for k in range(NKT):
        b = k % nb
        p.dma("Y", f"lnx{b}", xf[b][:, :], x_dram[128 * k:128 * (k + 1), :],
              writes=(f"xf{b}",))
        for h in range(2):
            sl = slice(512 * h, 512 * (h + 1))
            osl = slice(1024 * k + 512 * h, 1024 * k + 512 * (h + 1))
            p.op("V", lambda v, b=b, sl=sl: v.tensor_sub(
                t1[b][:, sl], xf[b][:, sl], pb[4 + (sl.start // 512)][:, :]),
                reads=(f"xf{b}", f"pb{4 + h}"), writes=(f"t1{b}",))
            if want_f32:
                p.op("V", lambda v, b=b, sl=sl, osl=osl, h=h: v.tensor_mul(
                    c["xlnf"][b][:, sl], t1[b][:, sl], pb[6 + h][:, :]),
                    reads=(f"t1{b}", f"pb{6 + h}"), writes=(f"xlnf_s{b}",))
                p.op("V", lambda v, b=b, sl=sl, osl=osl: v.tensor_copy(c["xln"][:, osl], c["xlnf"][b][:, sl]),
                     reads=(f"xlnf_s{b}",), writes=(f"xln_{k}",))
            else:
                p.op("V", lambda v, b=b, sl=sl, osl=osl, h=h: v.tensor_mul(
                    c["xln"][:, osl], t1[b][:, sl], pb[6 + h][:, :]),
                    reads=(f"t1{b}", f"pb{6 + h}"), writes=(f"xln_{k}",))
        if want_f32 and f32_sink is not None:
            f32_sink(k, b, f"xlnf_s{b}")


def xln_reads(k):
    return (f"xln_{k}",)


def ffn_io(p, sfx="", with_x=True):
    d = dict(wimg=p.din("wimg" + sfx, [NKT, 128, NKT * 128], F32),
             bbi=p.din("bbi" + sfx, [128, NKT], F32),
             wr=p.din("wr" + sfx, [128, NKT * 8], F32),
             br=p.din("br" + sfx, [8, 1], F32),
             aimg=p.din("aimg" + sfx, [128, NKT * 128], F32),
             bstk=p.din("bstk" + sfx, [128, NKT * 128], F32),
             patw=p.din("patw" + sfx, [8, 128], F32),
             idf=p.din("idf" + sfx, [128, 128], F32))
    if with_x:
        d["x"] = p.din("x" + sfx, [DM, TOK], F32)
        d["xo"] = p.dout("xo" + sfx, [DM, TOK], F32)
    return d


def build_ffn(variant, debug=False):
    p = Prog(f"ffn_{variant}_dbg" if debug else f"ffn_{variant}")
    io = ffn_io(p)
    croot = alloc_ln_root(p)
    emit_ln_consts(p, croot, p.din("onescol", [128, 1], BF16), p.din("onesrow", [1, 128], F32))
    emit_ffn(p, croot, io, variant, debug)
    return p


def emit_ffn(p, croot, io, variant, debug=False):
    p.scope_open()
    x, wimg, bbi, wr, br = io["x"], io["wimg"], io["bbi"], io["wr"], io["br"]
    aimg, bstk, patw, idf, xo = io["aimg"], io["bstk"], io["patw"], io["idf"], io["xo"]
    if debug:
        d_hsb = p.dout("d_hsb", [128, TOK], F32)
        d_lgs = p.dout("d_lgs", [8, TOK], F32)
        d_wT = p.dout("d_wT", [8, TOK], F32)
        d_gsb = p.dout("d_gsb", [128, TOK], F32)
        d_y0 = p.dout("d_y0", [128, TOK], F32)
        d_xln0 = p.dout("d_xln0", [128, TOK], F32)
        d_ztok = p.dout("d_ztok", [128, 64], F32)
        d_wtok = p.dout("d_wtok", [128, 64], F32)

    c = stage_ln(p, croot)
    pb = c["pb"]

    wt = [p.sb(f"wt{b}", [128, NKT * 128], BF16) for b in range(2)]
    at = p.sb("at", [128, NKT * 128], BF16)
    wrt = p.sb("wrt", [128, NKT * 8], BF16)
    bst = p.sb("bst", [128, NKT * 128], BF16)
    patt = p.sb("patt", [8, 128], F32)
    idt = p.sb("idt", [128, 128], F32)
    brt = p.sb("brt", [8, 1], F32)
    bbs = p.sb("bbs", [128, NKT], F32)
    hsb = p.sb("hsb", [128, TOK], F32)
    gsb = p.sb("gsb", [128, TOK], BF16)
    lgs = p.sb("lgs", [128, TOK], F32)
    ztok = p.sb("ztok", [128, 64], F32)
    wtokf = p.sb("wtokf", [128, 64], F32)
    wT = p.sb("wT", [8, TOK], F32)
    rr = [p.sb(f"rr{i}", [128, 8], F32) for i in range(4)]
    ysb = [p.sb(f"ysb{h}", [128, 512], F32) for h in range(2)]
    xr = [p.sb(f"xr{b}", [128, TOK], F32) for b in range(2)]

    emit_ln(p, c, x)
    xln = c["xln"]

    p.dma("G", "swa", at[:, :], aimg, writes=("at",))
    p.dma("G", "swr", wrt[:, :], wr, writes=("wrt",))
    p.dma("G", "swb", bst[:, :], bstk, writes=("bst",))
    p.dma("Y", "cp", patt[:, :], patw, writes=("patt",))
    p.dma("Y", "cid", idt[:, :], idf, writes=("idt",))
    p.dma("Y", "cbr", brt[:, :], br, writes=("brt",))
    p.dma("Y", "cbb", bbs[:, :], bbi, writes=("bbs",))

    for h in range(2):
        for k in range(NKT):
            p.op("T", lambda t, h=h, k=k: t.matmul(
                pb[h][:, :], at[:, 128 * k:128 * (k + 1)],
                xln[:, 1024 * k + 512 * h:1024 * k + 512 * (h + 1)],
                start=(k == 0), stop=(k == NKT - 1)),
                reads=("at",) + xln_reads(k), writes=(f"pb{h}",))
        p.op("V", lambda v, h=h: v.tensor_copy(hsb[:, 512 * h:512 * (h + 1)], pb[h][:, :]),
             reads=(f"pb{h}",), writes=("hsb",))
    for h in range(2):
        for k in range(NKT):
            p.op("T", lambda t, h=h, k=k: t.matmul(
                pb[2 + h][0:8, :], wrt[:, 8 * k:8 * (k + 1)],
                xln[:, 1024 * k + 512 * h:1024 * k + 512 * (h + 1)],
                start=(k == 0), stop=(k == NKT - 1)),
                reads=("wrt",) + xln_reads(k), writes=(f"pb{2 + h}",))
        p.op("V", lambda v, h=h: v.tensor_scalar(lgs[0:8, 512 * h:512 * (h + 1)],
             pb[2 + h][0:8, :], brt[:, 0:1], None, ALU.add),
             reads=(f"pb{2 + h}", "brt"), writes=("lgs",))

    for j in range(8):
        p.op("T", lambda t, j=j: t.transpose(pb[4][:, 0:128], lgs[:, 128 * j:128 * (j + 1)], idt[:, :]),
             reads=("lgs", "idt"), writes=("pb4",))
        p.op("V", lambda v, j=j: v.tensor_copy(ztok[:, 8 * j:8 * (j + 1)], pb[4][:, 0:8]),
             reads=("pb4",), writes=(f"ztok{j}",))
        z = ztok[:, 8 * j:8 * (j + 1)]
        wv = wtokf[:, 8 * j:8 * (j + 1)]
        if variant == "l0":
            p.op("V", lambda v, z=z: v.reduce_max(rr[0][:, 0:1], z, AX.X, negate=True),
                 reads=(f"ztok{j}",), writes=("rr0",))
            p.op("V", lambda v, z=z: v.tensor_scalar(rr[1][:, 0:8], z, rr[0][:, 0:1], None, ALU.add),
                 reads=(f"ztok{j}",), sreads=("rr0",), writes=("rr1",))
            p.op("V", lambda v: v.tensor_scalar(rr[2][:, 0:8], rr[1][:, 0:8], 0.0, -1e30, ALU.is_ge, ALU.mult),
                 reads=("rr1",), writes=("rr2",))
            p.op("V", lambda v: v.tensor_add(rr[2][:, 0:8], rr[1][:, 0:8], rr[2][:, 0:8]),
                 reads=("rr1", "rr2"), writes=("rr2",))
            p.op("V", lambda v: v.reduce_max(rr[3][:, 0:1], rr[2][:, 0:8], AX.X),
                 reads=("rr2",), writes=("rr3",))
            p.op("V", lambda v: v.tensor_scalar(rr[3][:, 0:8], rr[1][:, 0:8], rr[3][:, 0:1], None, ALU.is_ge),
                 reads=("rr1",), sreads=("rr3",), writes=("rr3",))
            p.op("S", lambda s: s.activation(rr[1][:, 0:8], rr[1][:, 0:8], AF.Exp),
                 reads=("rr1",), writes=("rr1",))
            p.op("V", lambda v: v.tensor_mul(rr[1][:, 0:8], rr[1][:, 0:8], rr[3][:, 0:8]),
                 reads=("rr1", "rr3"), writes=("rr1",))
            p.op("V", lambda v: v.reduce_sum(rr[2][:, 0:1], rr[1][:, 0:8], AX.X),
                 reads=("rr1",), writes=("rr2",))
            p.op("V", lambda v: v.reciprocal(rr[2][:, 0:1], rr[2][:, 0:1]),
                 reads=("rr2",), writes=("rr2",))
            p.op("V", lambda v, wv=wv: v.tensor_scalar(wv, rr[1][:, 0:8], rr[2][:, 0:1], 2.0, ALU.mult, ALU.mult),
                 reads=("rr1",), sreads=("rr2",), writes=("wtokf",))
        else:
            p.op("V", lambda v: v.memset(rr[0][:, 0:8], 0.0), writes=("rr0",))
            p.op("V", lambda v: v.memset(rr[1][:, 0:8], 0.0), writes=("rr1",))
            for jj in range(8):
                zj = ztok[:, 8 * j + jj:8 * j + jj + 1]
                p.op("V", lambda v, z=z, zj=zj: v.scalar_tensor_tensor(
                    rr[0][:, 0:8], z, zj, rr[0][:, 0:8], ALU.is_lt, ALU.add),
                    reads=("rr0",), sreads=(f"ztok{j}",), writes=("rr0",))
                p.op("V", lambda v, z=z, zj=zj: v.tensor_scalar(
                    rr[2][:, 0:8], z, zj, zj, ALU.is_le, ALU.mult),
                    sreads=(f"ztok{j}",), writes=("rr2",))
                p.op("V", lambda v: v.tensor_add(rr[1][:, 0:8], rr[1][:, 0:8], rr[2][:, 0:8]),
                     reads=("rr1", "rr2"), writes=("rr1",))
            p.op("V", lambda v: v.tensor_scalar(rr[0][:, 0:8], rr[0][:, 0:8], 1.0, None, ALU.add),
                 reads=("rr0",), writes=("rr0",))
            p.op("V", lambda v, z=z: v.tensor_mul(rr[2][:, 0:8], rr[0][:, 0:8], z),
                 reads=("rr0", f"ztok{j}"), writes=("rr2",))
            p.op("V", lambda v: v.scalar_tensor_tensor(rr[2][:, 0:8], rr[2][:, 0:8], 1.0, rr[1][:, 0:8], ALU.add, ALU.subtract),
                 reads=("rr2", "rr1"), writes=("rr2",))
            p.op("V", lambda v: v.tensor_scalar(rr[2][:, 0:8], rr[2][:, 0:8], 0.0, None, ALU.is_gt),
                 reads=("rr2",), writes=("rr2",))
            p.op("V", lambda v, z=z: v.tensor_mul(rr[3][:, 0:8], rr[2][:, 0:8], z),
                 reads=("rr2", f"ztok{j}"), writes=("rr3",))
            p.op("V", lambda v: v.reduce_sum(rr[3][:, 0:1], rr[3][:, 0:8], AX.X),
                 reads=("rr3",), writes=("rr3",))
            p.op("V", lambda v: v.reduce_sum(rr[2][:, 0:1], rr[2][:, 0:8], AX.X),
                 reads=("rr2",), writes=("rr2",))
            p.op("V", lambda v: v.reciprocal(rr[2][:, 0:1], rr[2][:, 0:1]),
                 reads=("rr2",), writes=("rr2",))
            p.op("V", lambda v: v.tensor_scalar(rr[3][:, 0:1], rr[3][:, 0:1], 1.0, None, ALU.subtract),
                 reads=("rr3",), writes=("rr3",))
            p.op("V", lambda v: v.tensor_mul(rr[3][:, 0:1], rr[3][:, 0:1], rr[2][:, 0:1]),
                 reads=("rr3", "rr2"), writes=("rr3",))
            p.op("V", lambda v, z=z, wv=wv: v.tensor_scalar(wv, z, rr[3][:, 0:1], 0.0, ALU.subtract, ALU.max),
                 reads=(f"ztok{j}",), sreads=("rr3",), writes=("wtokf",))
    for j in range(8):
        p.op("T", lambda t, j=j: t.transpose(pb[5][0:8, 0:128], wtokf[:, 8 * j:8 * (j + 1)], idt[:, :]),
             reads=("wtokf", "idt"), writes=("pb5",))
        p.op("V", lambda v, j=j: v.tensor_copy(wT[:, 128 * j:128 * (j + 1)], pb[5][0:8, 0:128]),
             reads=("pb5",), writes=("wT",))

    for h in range(2):
        p.op("T", lambda t, h=h: t.matmul(pb[6 + h][:, :], patt[:, :],
             wT[:, 512 * h:512 * (h + 1)], start=True, stop=True),
             reads=("patt", "wT"), writes=(f"pb{6 + h}",))
        p.op("V", lambda v, h=h: v.tensor_mul(gsb[:, 512 * h:512 * (h + 1)],
             hsb[:, 512 * h:512 * (h + 1)], pb[6 + h][:, :]),
             reads=("hsb", f"pb{6 + h}"), writes=("gsb",))

    if debug:
        p.dma("Y", "dbg8", d_ztok, ztok[:, :], reads=tuple(f"ztok{j}" for j in range(8)), out_sem=True)
        p.dma("Y", "dbg9", d_wtok, wtokf[:, :], reads=("wtokf",), out_sem=True)
        dbf = p.sb("dbf", [128, TOK], F32)
        p.dma("Y", "dbg1", d_hsb, hsb[:, :], reads=("hsb",), out_sem=True)
        p.dma("Y", "dbg2", d_lgs, lgs[0:8, :], reads=("lgs",), out_sem=True)
        p.dma("Y", "dbg3", d_wT, wT[:, :], reads=("wT",), out_sem=True)
        p.op("V", lambda v: v.tensor_copy(dbf[:, :], gsb[:, :]), reads=("gsb",), writes=("dbf",))
        p.dma("Y", "dbg4", d_gsb, dbf[:, :], reads=("dbf",), out_sem=True)
        p.op("V", lambda v: v.tensor_copy(dbf[:, :], xln[:, 0:TOK]), reads=("xln_0",), writes=("dbf2",))
        p.dma("Y", "dbg5", d_xln0, dbf[:, :], reads=("dbf2",), out_sem=True)

    for m in range(NKT):
        b = m % 2
        p.dma("G", f"w{b}", wt[b][:, :], wimg[m], writes=(f"wt{b}",))
        for h in range(2):
            bank = (m % 2) * 2 + h
            for k in range(NKT):
                p.op("T", lambda t, b=b, h=h, k=k, bank=bank: t.matmul(
                    pb[bank][:, :], wt[b][:, 128 * k:128 * (k + 1)],
                    xln[:, 1024 * k + 512 * h:1024 * k + 512 * (h + 1)],
                    start=(k == 0), stop=False),
                    reads=(f"wt{b}",) + xln_reads(k), writes=(f"pb{bank}",))
            p.op("T", lambda t, h=h, m=m, bank=bank: t.matmul(
                pb[bank][:, :], bst[:, 128 * m:128 * (m + 1)],
                gsb[:, 512 * h:512 * (h + 1)], start=False, stop=True),
                reads=("bst", "gsb"), writes=(f"pb{bank}",))
            p.op("V", lambda v, h=h, m=m, bank=bank: v.tensor_scalar(
                ysb[h][:, :], pb[bank][:, :], bbs[:, m:m + 1], None, ALU.add),
                reads=(f"pb{bank}", "bbs"), writes=(f"ysb{h}",))
        p.dma("Y", f"xr{b}", xr[b][:, :], x[128 * m:128 * (m + 1), :], writes=(f"xr{b}",))
        for h in range(2):
            p.op("V", lambda v, b=b, h=h: v.tensor_add(
                xr[b][:, 512 * h:512 * (h + 1)], xr[b][:, 512 * h:512 * (h + 1)], ysb[h][:, :]),
                reads=(f"xr{b}", f"ysb{h}"), writes=(f"xr{b}",))
        p.dma("Y", f"xo{b}", xo[128 * m:128 * (m + 1), :], xr[b][:, :],
              reads=(f"xr{b}",), out_sem=True)
        if debug and m == 0:
            p.dma("Y", "dbg6", d_y0[:, 0:512], ysb[0][:, :], reads=("ysb0",), out_sem=True)
            p.dma("Y", "dbg7", d_y0[:, 512:1024], ysb[1][:, :], reads=("ysb1",), out_sem=True)
    p.scope_close()


def host_ffn_consts(variant, W_base, b_base, W_router, b_router, lora_A, lora_B):
    MATW = np.ascontiguousarray(W_base.T)
    wimg = blocked_img(MATW)
    wr = small_img(np.ascontiguousarray(W_router.T))
    if variant == "l0":
        A = lora_A.transpose(2, 0, 1).reshape(DM, 128)
        Bm = lora_B.transpose(0, 2, 1).reshape(128, DM)
    else:
        A = lora_A.transpose(1, 0, 2).reshape(DM, 128)
        Bm = lora_B.reshape(128, DM)
    aimg = blocked_img(np.ascontiguousarray(A))[0]
    bstk = np.ascontiguousarray(Bm)
    patw = np.zeros((8, 128), dtype=np.float32)
    for e in range(8):
        patw[e, 16 * e:16 * (e + 1)] = 1
    return dict(
        wimg=wimg.astype(np.float32),
        bbi=np.ascontiguousarray(b_base.reshape(NKT, 128).T).astype(np.float32),
        wr=wr.astype(np.float32),
        br=(b_router if b_router is not None else np.zeros(8)).reshape(8, 1).astype(np.float32),
        aimg=aimg.astype(np.float32), bstk=bstk.astype(np.float32), patw=patw,
        idf=np.eye(128, dtype=np.float32),
        onescol=np.ones((128, 1), dtype=BF),
        onesrow=np.ones((1, 128), dtype=np.float32),
    )




def build_attn_pre(wdt="f32"):
    p = Prog(f"attn_pre_{wdt}")
    io = dict(x=p.din("x", [DM, TOK], F32),
              wimg=p.din("wimg", [48, 128, NKT * 128], F32 if wdt == "f32" else BF16),
              qkvp=p.dout("qkvp", [6144, TOK], BF16))
    croot = alloc_ln_root(p)
    emit_ln_consts(p, croot, p.din("onescol", [128, 1], BF16), p.din("onesrow", [1, 128], F32))
    emit_attn_pre(p, croot, io)
    return p


def emit_attn_pre(p, croot, io):
    p.scope_open()
    x, wimg, qkvp = io["x"], io["wimg"], io["qkvp"]
    c = stage_ln(p, croot)
    wt = [p.sb(f"wt{b}", [128, NKT * 128], BF16) for b in range(2)]
    ob = [p.sb(f"ob{b}", [128, TOK], BF16) for b in range(2)]
    emit_ln(p, c, x)
    xln, pb = c["xln"], c["pb"]
    for m in range(48):
        b = m % 2
        p.dma("G", f"w{b}", wt[b][:, :], wimg[m], writes=(f"wt{b}",))
        for h in range(2):
            bank = (m % 2) * 2 + h
            for k in range(NKT):
                p.op("T", lambda t, b=b, h=h, k=k, bank=bank: t.matmul(
                    pb[bank][:, :], wt[b][:, 128 * k:128 * (k + 1)],
                    xln[:, 1024 * k + 512 * h:1024 * k + 512 * (h + 1)],
                    start=(k == 0), stop=(k == NKT - 1)),
                    reads=(f"wt{b}",) + xln_reads(k), writes=(f"pb{bank}",))
            p.op("S", lambda s, b=b, h=h, bank=bank: s.copy(
                ob[b][:, 512 * h:512 * (h + 1)], pb[bank][:, :]),
                reads=(f"pb{bank}",), writes=(f"ob{b}",))
        p.dma("Y", f"o{b}", qkvp[128 * m:128 * (m + 1), :], ob[b][:, :],
              reads=(f"ob{b}",), out_sem=True)
    p.scope_close()


NQT = 16


def build_attn_core():
    p = Prog("attn_core")
    io = dict(qkvm=p.din("qkvm", [6144, TOK], BF16),
              cosf=p.din("cosf", [128, SEQ], F32),
              sinf=p.din("sinf", [128, SEQ], F32),
              trimask=p.din("trimask", [4, 128, 512], BF16),
              onescol=p.din("onescol", [128, 1], BF16),
              onesrow=p.din("onesrow", [1, 128], F32),
              aout=p.dout("aout", [DM, TOK], BF16))
    emit_attn_core(p, io)
    return p


def emit_attn_core(p, io):
    p.scope_open()
    qkvm, cosf, sinf, trimask = io["qkvm"], io["cosf"], io["sinf"], io["trimask"]
    ones_col_d, ones_row_d, aout = io["onescol"], io["onesrow"], io["aout"]
    dend = io["dend"]

    qh = [p.sb(f"qh{b}", [128, SEQ], BF16) for b in range(2)]
    kh = p.sb("kh", [128, SEQ], BF16)
    qr = [p.sb(f"qr{b}", [128, SEQ], BF16) for b in range(2)]
    kr = p.sb("kr", [128, SEQ], BF16)
    vtok = p.sb("vtok", [128, SEQ], BF16)
    cost = p.sb("cost", [128, SEQ], BF16)
    sint = p.sb("sint", [128, SEQ], BF16)
    tmask = p.sb("tmask", [128, 4 * 512], BF16)
    ones_col = p.sb("ones_col2", [128, 1], BF16)
    ones_row_f = p.sb("ones_row2", [1, 128], F32)
    t2 = p.sb("t2", [128, SEQ // 4], BF16)
    den = [p.sb(f"den{i}", [1, 512], F32) for i in range(2)]
    ncols = p.sb("ncols", [1, 16], F32)
    sqn = [p.sb(f"sqn{i}", [128, 512], BF16) for i in range(2)]
    mrow = p.sb("mrow", [1, 8], F32)
    negm = p.sb("negm", [128, 2], F32)
    psb = [p.sb(f"psb{i}", [128, 1024], BF16) for i in range(2)]
    osb = [p.sb(f"osb{i}", [128, 512], BF16) for i in range(2)]
    ps_s = [p.psum(f"ps_s{i}", [128, 1024], F32) for i in range(2)]
    ps_av = [p.psum(f"ps_av{i}", [128, 512], F32) for i in range(2)]
    ps_d = [p.psum(f"ps_d{i}", [1, 512], F32) for i in range(2)]
    ps_n = ps_d[1]

    p.dma("G", "ccos", cost[:, :], cosf, writes=("cost",))
    p.dma("G", "csin", sint[:, :], sinf, writes=("sint",))
    p.dma("Y", "cmask", tmask[:, :].rearrange("p (j f) -> p j f", j=4), trimask.rearrange("j p f -> p j f"), writes=("tmask",))
    p.dma("Y", "cones", ones_col[:, :], ones_col_d, writes=("ones_col2",))
    p.dma("Y", "cones2", ones_row_f[:, :], ones_row_d, writes=("ones_row2",))

    def load_headrows(dst, dname, base, sem):
        for i in range(NCORE):
            p.dma("Y", sem, dst[:, 1024 * i:1024 * (i + 1)],
                  qkvm[768 * i + base:768 * i + base + 128, :], writes=(f"{dname}_{i}",))

    def rd(dname):
        return tuple(f"{dname}_{i}" for i in range(NCORE))

    def rope(dst_name, dst, src_name, src):
        srcr = rd(src_name)
        H = SEQ // 4
        for g in range(4):
            sl = slice(H * g, H * (g + 1))
            p.op("V", lambda v, sl=sl: v.tensor_mul(t2[0:64, :], src[64:128, sl], sint[64:128, sl]),
                 reads=srcr + ("sint",), writes=("t2a",))
            p.op("V", lambda v, sl=sl: v.tensor_mul(t2[64:128, :], src[0:64, sl], sint[0:64, sl]),
                 reads=srcr + ("sint",), writes=("t2b",))
            p.op("V", lambda v, sl=sl: v.tensor_mul(dst[:, sl], src[:, sl], cost[:, sl]),
                 reads=srcr + ("cost",), writes=(dst_name + "_m",))
            p.op("V", lambda v, sl=sl: v.tensor_add(dst[:, sl], dst[:, sl], t2[:, :]),
                 reads=(dst_name + "_m", "t2a", "t2b"), writes=(dst_name,))

    def norms(src_name, src, out_col):
        for i in range(NQT):
            b = i % 2
            p.op("S", lambda s, i=i, b=b: s.activation(sqn[b][:, :], src[:, 512 * i:512 * (i + 1)], AF.Square),
                 reads=(src_name,), writes=(f"sqn{b}",))
            p.op("T", lambda t, b=b: t.matmul(ps_d[b][0:1, :], ones_col[:, 0:1],
                 sqn[b][:, :], start=True, stop=True),
                 reads=(f"sqn{b}", "ones_col2"), writes=(f"ps_d{b}",))
            p.op("V", lambda v, i=i, b=b: v.reduce_max(ncols[0:1, i:i + 1], ps_d[b][0:1, :], AX.X),
                 reads=(f"ps_d{b}",), writes=("ncols",))
        p.op("V", lambda v, oc=out_col: v.reduce_max(mrow[0:1, oc:oc + 1], ncols[0:1, :], AX.X),
             reads=("ncols",), writes=(f"mcol{out_col}",))

    load_headrows(kh, "kh", 512, "lk")
    norms("kh", kh, 0)
    rope("kr", kr, "kh", kh)
    for i in range(64):
        p.dma("S", "lv", vtok[:, 128 * i:128 * (i + 1)],
              qkvm[768 * (i // 8) + 640:768 * (i // 8) + 768, 128 * (i % 8):128 * (i % 8 + 1)],
              transpose=True, writes=(f"vtok_{i // 8}",))

    def prep_head(hq):
        b = hq % 2
        load_headrows(qh[b], f"qh{b}", 128 * hq, f"lq{b}")
        norms(f"qh{b}", qh[b], 1)
        rope(f"qr{b}", qr[b], f"qh{b}", qh[b])
        p.op("V", lambda v: v.tensor_mul(mrow[0:1, 2:3], mrow[0:1, 0:1], mrow[0:1, 1:2]),
             reads=("mcol0", "mcol1"), writes=("m2c",))
        p.op("S", lambda s: s.activation(mrow[0:1, 3:4], mrow[0:1, 2:3], AF.Sqrt),
             reads=("m2c",), writes=("m3c",))
        p.op("V", lambda v: v.tensor_scalar(mrow[0:1, 4:5], mrow[0:1, 3:4], -0.08838834764831845, None, ALU.mult),
             reads=("m3c",), writes=("m4c",))
        p.op("T", lambda t: t.matmul(ps_av[1][:, 0:1], ones_row_f[0:1, :], mrow[0:1, 4:5], start=True, stop=True),
             reads=("m4c", "ones_row2"), writes=("ps_av1",))
        p.op("V", lambda v, b=b: v.tensor_copy(negm[:, b:b + 1], ps_av[1][:, 0:1]),
             reads=("ps_av1",), writes=(f"negm{b}",))

    prep_head(0)
    for hq in range(4):
        hb = hq % 2
        for qt in range(NQT):
            if qt == 8 and hq < 3:
                prep_head(hq + 1)
            ntk = 4 * qt + 4
            npair = ntk // 2

            def emit_sc(pr, qt=qt, hb=hb):
                sb = pr % 2
                for u in range(2):
                    i = 2 * pr + u
                    p.op("T", lambda t, i=i, qt=qt, sb=sb, u=u, hb=hb: t.matmul(
                        ps_s[sb][:, 512 * u:512 * (u + 1)], kr[:, 128 * i:128 * (i + 1)],
                        qr[hb][:, 512 * qt:512 * (qt + 1)], start=True, stop=True),
                        reads=("kr", f"qr{hb}"), writes=(f"ps_s{sb}_{u}",))
                p.op("S", lambda s, sb=sb, hb=hb: s.activation(psb[sb][:, :], ps_s[sb][:, :], AF.Exp,
                     bias=negm[:, hb:hb + 1], scale=0.08838834764831845),
                    reads=(f"ps_s{sb}_0", f"ps_s{sb}_1"), sreads=(f"negm{hb}",), writes=(f"psb{sb}",))
                for u in range(2):
                    i = 2 * pr + u
                    diag = i - 4 * qt
                    if diag >= 0:
                        p.op("V", lambda v, sb=sb, u=u, d=diag: v.tensor_mul(
                             psb[sb][:, 512 * u:512 * (u + 1)], psb[sb][:, 512 * u:512 * (u + 1)],
                             tmask[:, 512 * d:512 * (d + 1)]),
                             reads=(f"psb{sb}", "tmask"), writes=(f"psb{sb}",))

            def emit_dnav(pr, qt=qt, ntk=ntk):
                sb = pr % 2
                for u in range(2):
                    i = 2 * pr + u
                    p.op("T", lambda t, sb=sb, u=u, i=i, ntk=ntk, qt=qt: t.matmul(
                         ps_d[qt % 2][0:1, :], ones_col[:, 0:1],
                         psb[sb][:, 512 * u:512 * (u + 1)], start=(i == 0), stop=(i == ntk - 1)),
                         reads=(f"psb{sb}", "ones_col2"), writes=(f"ps_d{qt % 2}",))
                    p.op("T", lambda t, sb=sb, u=u, i=i, ntk=ntk, qt=qt: t.matmul(
                         ps_av[qt % 2][:, :],
                         vtok[:, 128 * i:128 * (i + 1)], psb[sb][:, 512 * u:512 * (u + 1)],
                         start=(i == 0), stop=(i == ntk - 1)),
                         reads=(f"psb{sb}", f"vtok_{i // 8}"), writes=(f"ps_av{qt % 2}",))

            for pr in range(npair):
                emit_sc(pr)
                if pr >= 1:
                    emit_dnav(pr - 1)
            emit_dnav(npair - 1)
            b2 = qt % 2
            p.op("V", lambda v, b2=b2: v.tensor_copy(osb[b2][:, :], ps_av[b2][:, :]),
                 reads=(f"ps_av{b2}",), writes=(f"osb{b2}",))
            p.op("V", lambda v, b2=b2: v.tensor_copy(den[b2][0:1, :], ps_d[b2][0:1, :]),
                 reads=(f"ps_d{b2}",), writes=(f"den{b2}",))
            j, half = qt // 2, qt % 2
            p.dma("Y", f"oa{qt % 2}", aout[512 * j + 128 * hq:512 * j + 128 * (hq + 1),
                  512 * half:512 * (half + 1)], osb[b2][:, :], reads=(f"osb{b2}",), out_sem=True)
            p.dma("Y", f"od{qt % 2}", dend[8 * j + 2 * hq + half:8 * j + 2 * hq + half + 1, :],
                  den[b2][0:1, :], reads=(f"den{b2}",), out_sem=True)
    p.scope_close()


def build_attn_post(wdt="f32"):
    p = Prog(f"attn_post_{wdt}")
    io = dict(x=p.din("x", [DM, TOK], F32),
              at=p.din("at", [DM, TOK], BF16),
              wimg=p.din("wimg", [NKT, 128, NKT * 128], F32 if wdt == "f32" else BF16),
              xo=p.dout("xo", [DM, TOK], F32))
    emit_attn_post(p, io)
    return p


def emit_post_rcp_prologue(p, io, att, pb):
    dA, dAr, onesrow = io["dA"], io["dAr"], io["onesrow"]
    denb = p.sb("denb", [64, 512], F32)
    denr = p.sb("denr", [64, 512], F32)
    rc1 = [p.sb(f"rc1_{b}", [1, 1024], F32) for b in range(2)]
    onr = p.sb("onr2", [1, 128], F32)
    p.dma("Y", "dld", denb[:, :], dA, writes=("denb",))
    p.dma("Y", "dld", onr[:, :], onesrow, writes=("onr2",))
    p.op("V", lambda v: v.reciprocal(denr[:, :], denb[:, :]),
         reads=("denb",), writes=("denr",))
    p.dma("Y", "dst", dAr.rearrange("b (g f) -> (b g) f", g=2), denr[:, :],
          reads=("denr",), writes=("dAr_d",))

    def scale_tile(k):
        b = k % 2
        p.dma("Y", f"rl{b}", rc1[b][:, :], dAr[k:k + 1, :],
              reads=("dAr_d",), writes=(f"rc1{b}",))
        for h in range(2):
            p.op("T", lambda t, b=b, h=h: t.matmul(pb[h][:, :], onr[0:1, :],
                 rc1[b][0:1, 512 * h:512 * (h + 1)], start=True, stop=True),
                 reads=(f"rc1{b}", "onr2"), writes=(f"pb{h}",))
            p.op("V", lambda v, k=k, h=h: v.tensor_mul(
                 att[:, 1024 * k + 512 * h:1024 * k + 512 * (h + 1)],
                 att[:, 1024 * k + 512 * h:1024 * k + 512 * (h + 1)], pb[h][:, :]),
                 reads=(f"att{k}", f"pb{h}"), writes=(f"att{k}",))
    return scale_tile


def emit_attn_post(p, io):
    p.scope_open()
    x, at, wimg, xo = io["x"], io["at"], io["wimg"], io["xo"]
    wt = [p.sb(f"wt{b}", [128, NKT * 128], BF16) for b in range(2)]
    att = p.sb("att", [128, NKT * TOK], BF16)
    xr = [p.sb(f"xr{b}", [128, TOK], F32) for b in range(2)]
    pb = [p.psum(f"pb{i}", [128, 512], F32) for i in range(4)]
    scale_tile = emit_post_rcp_prologue(p, io, att, pb)
    for k in range(NKT):
        p.dma("Y", f"a{k % 2}", att[:, 1024 * k:1024 * (k + 1)], at[128 * k:128 * (k + 1), :],
              writes=(f"att{k}",))
        scale_tile(k)
    for m in range(NKT):
        b = m % 2
        p.dma("G", f"w{b}", wt[b][:, :], wimg[m], writes=(f"wt{b}",))
        for h in range(2):
            bank = (m % 2) * 2 + h
            for k in range(NKT):
                p.op("T", lambda t, b=b, h=h, k=k, bank=bank: t.matmul(
                    pb[bank][:, :], wt[b][:, 128 * k:128 * (k + 1)],
                    att[:, 1024 * k + 512 * h:1024 * k + 512 * (h + 1)],
                    start=(k == 0), stop=(k == NKT - 1)),
                    reads=(f"wt{b}", f"att{k}"), writes=(f"pb{bank}",))
        p.dma("Y", f"xr{b}", xr[b][:, :], x[128 * m:128 * (m + 1), :], writes=(f"xr{b}",))
        for h in range(2):
            bank = (m % 2) * 2 + h
            p.op("V", lambda v, b=b, h=h, bank=bank: v.tensor_add(
                xr[b][:, 512 * h:512 * (h + 1)], xr[b][:, 512 * h:512 * (h + 1)], pb[bank][:, :]),
                reads=(f"xr{b}", f"pb{bank}"), writes=(f"xr{b}",))
        p.dma("Y", f"xo{b}", xo[128 * m:128 * (m + 1), :], xr[b][:, :],
              reads=(f"xr{b}",), out_sem=True)
    p.scope_close()


_PROG_CACHE = {}


def get_prog(key, builder):
    if key not in _PROG_CACHE:
        _PROG_CACHE[key] = builder().finish()
    return _PROG_CACHE[key]


def run_ffn(variant, xT_shards, consts):
    nc = get_prog(f"ffn_{variant}", lambda: build_ffn(variant))
    in_maps = [dict(consts, x=np.ascontiguousarray(s)) for s in xT_shards]
    res = run_bass_kernel_spmd(nc, in_maps, core_ids=list(range(NCORE)))
    return [r["xo"] for r in res.results]




def host_qkv_mat(Wq_mat, Wk_mat, Wv_mat):
    cols = []
    for c in range(NCORE):
        cols.append(Wq_mat[:, 512 * c:512 * (c + 1)])
        cols.append(Wk_mat[:, 128 * c:128 * (c + 1)])
        cols.append(Wv_mat[:, 128 * c:128 * (c + 1)])
    return np.concatenate(cols, axis=1)


def host_rope_tables(kind):
    if kind == "l0":
        HEAD_DIM = 128; ROPE_THETA = 500000.0; ROPE_FACTOR = 8.0
        HIGH_FF = 4.0; LOW_FF = 1.0; ORIG_MAX_POS = 8192
        inv = 1.0 / ROPE_THETA ** (np.arange(0, HEAD_DIM, 2, dtype=np.float32) / HEAD_DIM)
        wavelen = 2.0 * np.pi / inv
        low_wl = ORIG_MAX_POS / LOW_FF
        high_wl = ORIG_MAX_POS / HIGH_FF
        inv_scaled = np.where(wavelen > low_wl, inv / ROPE_FACTOR, inv)
        smooth = (ORIG_MAX_POS / wavelen - LOW_FF) / (HIGH_FF - LOW_FF)
        smoothed = (1.0 - smooth) / ROPE_FACTOR * inv + smooth * inv
        is_medium = (wavelen <= low_wl) & (wavelen >= high_wl)
        inv_freq = np.where(is_medium, smoothed, inv_scaled).astype(np.float32)
    else:
        inv_freq = (1.0 / 10000.0 ** (np.arange(0, 128, 2, dtype=np.float32) / 128)).astype(np.float32)
    pos = np.arange(SEQ, dtype=np.float32)
    freqs = pos[:, None] * inv_freq[None, :]
    emb = np.concatenate([freqs, freqs], axis=-1)
    cos = np.cos(emb).astype(np.float32).T
    sin = np.sin(emb).astype(np.float32).T
    sinf = np.concatenate([sin[64:128], -sin[0:64]], axis=0)
    return np.ascontiguousarray(cos), np.ascontiguousarray(sinf)


def host_trimask():
    m = np.zeros((4, 128, 512), dtype=BF)
    for d in range(4):
        for pp in range(128):
            m[d, pp, 128 * d + pp:] = 1.0
    return m


def host_attn_consts(kind, Wq, Wk, Wv, Wo):
    if kind == "l0":
        mq, mk, mv, mo = Wq.T, Wk.T, Wv.T, Wo.T
    else:
        mq, mk, mv, mo = Wq, Wk, Wv, Wo
    qkv = host_qkv_mat(np.ascontiguousarray(mq), np.ascontiguousarray(mk), np.ascontiguousarray(mv))
    cos, sinf = host_rope_tables("l0" if kind == "l0" else "l1")
    dt = np.float32 if qkv.dtype == np.float32 else BF
    pre = dict(wimg=blocked_img(qkv).astype(dt),
               onescol=np.ones((128, 1), dtype=BF),
               onesrow=np.ones((1, 128), dtype=np.float32))
    core = dict(cosf=cos, sinf=sinf, trimask=host_trimask(),
                onescol=np.ones((128, 1), dtype=BF),
                onesrow=np.ones((1, 128), dtype=np.float32))
    post = dict(wimg=blocked_img(np.ascontiguousarray(mo)).astype(dt))
    return pre, core, post


def run_attn(kind, xT_shards, consts3, wdt="f32"):
    pre_c, core_c, post_c = consts3
    nc1 = get_prog(f"attn_pre_{wdt}", lambda: build_attn_pre(wdt))
    res1 = run_bass_kernel_spmd(nc1, [dict(pre_c, x=np.ascontiguousarray(s)) for s in xT_shards],
                                core_ids=list(range(NCORE)))
    qkvp = [r["qkvp"] for r in res1.results]
    nc2 = get_prog("attn_core", lambda: build_attn_core())
    in2 = []
    for c in range(NCORE):
        mine = np.concatenate([np.asarray(qkvp[i])[768 * c:768 * (c + 1)] for i in range(NCORE)], axis=0)
        in2.append(dict(core_c, qkvm=np.ascontiguousarray(mine)))
    res2 = run_bass_kernel_spmd(nc2, in2, core_ids=list(range(NCORE)))
    aout = [r["aout"] for r in res2.results]
    nc3 = get_prog(f"attn_post_{wdt}", lambda: build_attn_post(wdt))
    in3 = []
    for c in range(NCORE):
        at = np.concatenate([np.asarray(aout[i])[512 * c:512 * (c + 1)] for i in range(NCORE)], axis=0)
        in3.append(dict(post_c, x=np.ascontiguousarray(xT_shards[c]), at=np.ascontiguousarray(at)))
    res3 = run_bass_kernel_spmd(nc3, in3, core_ids=list(range(NCORE)))
    return [r["xo"] for r in res3.results]




def build_ssm_pre():
    p = Prog("ssm_pre")
    io = dict(x=p.din("x", [DM, TOK], F32),
              wimg=p.din("wimg", [2, 128, NKT * 128], F32),
              xlno=p.dout("xlno", [DM, TOK], F32),
              xpo=p.dout("xpo", [160, TOK], F32))
    croot = alloc_ln_root(p)
    emit_ln_consts(p, croot, p.din("onescol", [128, 1], BF16), p.din("onesrow", [1, 128], F32))
    emit_ssm_pre(p, croot, io)
    return p


def emit_ssm_pre(p, croot, io):
    p.scope_open()
    x, wimg, xlno, xpo = io["x"], io["wimg"], io["xlno"], io["xpo"]
    c = stage_ln(p, croot, want_f32=True)
    wt = [p.sb(f"wt{b}", [128, NKT * 128], BF16) for b in range(2)]
    xp1 = p.sb("xp1", [128, TOK], F32)
    xp2 = p.sb("xp2", [32, TOK], F32)
    def sink(k, b, res):
        p.dma("Y", f"xl{b}", xlno[128 * k:128 * (k + 1), :],
              c["xlnf"][b][:, :], reads=(res,), out_sem=True)
    emit_ln(p, c, x, want_f32=True, f32_sink=sink)
    xln, pb = c["xln"], c["pb"]
    for b in range(2):
        p.dma("G", f"w{b}", wt[b][:, :], wimg[b], writes=(f"wt{b}",))
    for h in range(2):
        for k in range(NKT):
            p.op("T", lambda t, h=h, k=k: t.matmul(
                pb[h][:, :], wt[0][:, 128 * k:128 * (k + 1)],
                xln[:, 1024 * k + 512 * h:1024 * k + 512 * (h + 1)],
                start=(k == 0), stop=(k == NKT - 1)),
                reads=("wt0",) + xln_reads(k), writes=(f"pb{h}",))
            p.op("T", lambda t, h=h, k=k: t.matmul(
                pb[2 + h][0:32, :], wt[1][:, 128 * k:128 * k + 32],
                xln[:, 1024 * k + 512 * h:1024 * k + 512 * (h + 1)],
                start=(k == 0), stop=(k == NKT - 1)),
                reads=("wt1",) + xln_reads(k), writes=(f"pb{2 + h}",))
        p.op("V", lambda v, h=h: v.tensor_copy(xp1[:, 512 * h:512 * (h + 1)], pb[h][:, :]),
             reads=(f"pb{h}",), writes=("xp1",))
        p.op("V", lambda v, h=h: v.tensor_copy(xp2[:, 512 * h:512 * (h + 1)], pb[2 + h][0:32, :]),
             reads=(f"pb{2 + h}",), writes=("xp2",))
    p.dma("Y", "xpo1", xpo[0:128, :], xp1[:, :], reads=("xp1",), out_sem=True)
    p.dma("Y", "xpo2", xpo[128:160, :], xp2[0:32, :], reads=("xp2",), out_sem=True)
    p.scope_close()


def build_ssm_core():
    p = Prog("ssm_core")
    io = dict(xslc=p.din("xslc", [DM, TOK], F32),
              xpf=p.din("xpf", [1280, TOK], F32),
              alog=p.din("alog", [128, 64], F32),
              dtw=p.din("dtw", [128, 512], F32),
              dtb=p.din("dtb", [128, 4], F32),
              patd=p.din("patd", [128, 16 * 128], BF16),
              patb=p.din("patb", [16, 128], BF16),
              paty=p.din("paty", [128, 16 * 128], BF16),
              yT=p.dout("yT", [DM, TOK], BF16))
    emit_ssm_core(p, io)
    return p


def emit_ssm_core(p, io):
    p.scope_open()
    xslc, xpf, alog, dtw, dtb = io["xslc"], io["xpf"], io["alog"], io["dtw"], io["dtb"]
    patd_d, patb_d, paty_d, yT = io["patd"], io["patb"], io["paty"], io["yT"]

    aimg = p.sb("aimg", [128, 64], F32)
    atmp = p.sb("atmp", [128, 64], F32)
    dtin = p.sb("dtin", [128, SEQ], BF16)
    btc = p.sb("btc", [16, 512], BF16)
    dtw_sb = p.sb("dtw_sb", [128, 512], BF16)
    dtb_sb = p.sb("dtb_sb", [128, 4], F32)
    patd_sb = p.sb("patd_sb", [128, 16 * 128], BF16)
    patb_sb = p.sb("patb_sb", [16, 128], BF16)
    paty_sb = p.sb("paty_sb", [128, 16 * 128], BF16)
    brep = p.sb("brep", [128, SEQ], BF16)
    crep = p.sb("crep", [128, SEQ], BF16)
    x_sb = p.sb("x_sb", [128, SEQ], BF16)
    dt_sb = p.sb("dt_sb", [128, SEQ], BF16)
    u_sb = p.sb("u_sb", [128, SEQ], BF16)
    sp = [[p.sb(f"sp{i}_{j}", [128, 1024], F32) for i in range(3)] for j in range(2)]
    state = p.sb("state", [128, 16], F32)
    dA = [p.sb(f"dA{b}", [128, 1024], F32) for b in range(2)]
    dBx = [p.sb(f"dBx{b}", [128, 1024], F32) for b in range(2)]
    h_sb = [p.sb(f"h_sb{b}", [128, 1024], F32) for b in range(2)]
    hc = [p.sb(f"hc{b}", [128, 1024], BF16) for b in range(2)]
    ybf = p.sb("ybf", [128, 1024], BF16)
    psA = p.psum("psA", [128, 1024], F32)
    psU = p.psum("psU", [128, 1024], F32)
    psY = p.psum("psY", [128, 1024], F32)
    psBC = p.psum("psBC", [128, 1024], F32)

    p.dma("Y", "cal", aimg[:, :], alog, writes=("alog_raw",))
    p.op("S", lambda s: s.activation(atmp[:, :], aimg[:, :], AF.Exp),
         reads=("alog_raw",), writes=("atmp",))
    p.op("V", lambda v: v.tensor_scalar(aimg[:, :], atmp[:, :], -1.0, None, ALU.mult),
         reads=("atmp",), writes=("aimg",))
    p.dma("G", "cdw", dtw_sb[:, :], dtw, writes=("dtw_sb",))
    p.dma("Y", "cdb", dtb_sb[:, :], dtb, writes=("dtb_sb",))
    p.dma("Y", "cpd", patd_sb[:, :], patd_d, writes=("patd_sb",))
    p.dma("Y", "cpb", patb_sb[:, :], patb_d, writes=("patb_sb",))
    p.dma("Y", "cpy", paty_sb[:, :], paty_d, writes=("paty_sb",))
    for i in range(NCORE):
        p.dma("G", "cdi", dtin[:, 1024 * i:1024 * (i + 1)], xpf[160 * i:160 * i + 128, :],
              writes=(f"dtin_{i}",))
    dtin_r = tuple(f"dtin_{i}" for i in range(NCORE))

    for rbase, dst, dname in ((128, brep, "brep"), (144, crep, "crep")):
        for i in range(NQT):
            rk, half = i // 2, i % 2
            p.dma("G", "cbt", btc[:, :],
                  xpf[160 * rk + rbase:160 * rk + rbase + 16, 512 * half:512 * (half + 1)],
                  writes=("btc",))
            p.op("T", lambda t: t.matmul(
                psBC[:, 0:512], patb_sb[:, :], btc[0:16, :],
                start=True, stop=True),
                reads=("btc", "patb_sb"), writes=("psBC",))
            p.op("V", lambda v, i=i, dst=dst: v.tensor_copy(dst[:, 512 * i:512 * (i + 1)], psBC[:, 0:512]),
                 reads=("psBC",), writes=(dname,))

    for dsub in range(4):
        for i in range(NCORE):
            p.dma("G", "cx", x_sb[:, 1024 * i:1024 * (i + 1)],
                  xslc[512 * i + 128 * dsub:512 * i + 128 * (dsub + 1), :],
                  writes=(f"xbuf_{i}",))
        x_r = tuple(f"xbuf_{i}" for i in range(NCORE))
        for t8 in range(8):
            j = t8 % 2
            s0, s1, s2 = sp[j]
            for hh in range(2):
                sl = slice(1024 * t8 + 512 * hh, 1024 * t8 + 512 * (hh + 1))
                p.op("T", lambda t, dsub=dsub, sl=sl, hh=hh: t.matmul(
                    psBC[:, 512 * hh:512 * (hh + 1)], dtw_sb[:, 128 * dsub:128 * (dsub + 1)],
                    dtin[:, sl], start=True, stop=True),
                    reads=dtin_r + ("dtw_sb",), writes=("psBC",))
            p.op("V", lambda v, dsub=dsub, s0=s0: v.tensor_scalar(s0[:, :], psBC[:, :],
                 dtb_sb[:, dsub:dsub + 1], None, ALU.add),
                 reads=("psBC",), sreads=("dtb_sb",), writes=(f"sp0_{j}",))
            p.op("S", lambda s, s0=s0, s1=s1: s.activation(s1[:, :], s0[:, :], AF.Abs),
                 reads=(f"sp0_{j}",), writes=(f"sp1_{j}",))
            p.op("S", lambda s, s1=s1: s.activation(s1[:, :], s1[:, :], AF.Exp, scale=-1.0),
                 reads=(f"sp1_{j}",), writes=(f"sp1_{j}",))
            p.op("V", lambda v, s1=s1: v.tensor_scalar(s1[:, :], s1[:, :], 1.0, None, ALU.add),
                 reads=(f"sp1_{j}",), writes=(f"sp1_{j}",))
            p.op("S", lambda s, s1=s1: s.activation(s1[:, :], s1[:, :], AF.Ln),
                 reads=(f"sp1_{j}",), writes=(f"sp1_{j}",))
            p.op("V", lambda v, s0=s0, s2=s2: v.tensor_scalar(s2[:, :], s0[:, :], 0.0, None, ALU.max),
                 reads=(f"sp0_{j}",), writes=(f"sp2_{j}",))
            p.op("V", lambda v, t8=t8, s1=s1, s2=s2: v.tensor_add(
                 dt_sb[:, 1024 * t8:1024 * (t8 + 1)], s1[:, :], s2[:, :]),
                 reads=(f"sp1_{j}", f"sp2_{j}"), writes=(f"dtt_{t8}",))
        dt_r = tuple(f"dtt_{t8}" for t8 in range(8))
        p.op("V", lambda v: v.tensor_mul(u_sb[:, :], dt_sb[:, :], x_sb[:, :]),
             reads=dt_r + x_r, writes=("u_sb",))
        p.op("V", lambda v: v.memset(state[:, :], 0.0), writes=("state",))
        def emit_red(grp):
            gsl = slice(128 * grp, 128 * (grp + 1))
            gb = grp % 2
            for h in range(2):
                p.op("T", lambda t, grp=grp, h=h, gsl=gsl, gb=gb: t.matmul(
                    psY[:, 512 * h:512 * (h + 1)], paty_sb[:, gsl], hc[gb][:, 512 * h:512 * (h + 1)],
                    start=(grp == 0), stop=(grp == 15)),
                    reads=(f"hc{gb}", "paty_sb"), writes=(f"psY{h}",))

        for tcB in range(8):
            csl = slice(1024 * tcB, 1024 * (tcB + 1))
            for grp in range(16):
                gsl = slice(128 * grp, 128 * (grp + 1))
                for h in range(2):
                    hsl = slice(1024 * tcB + 512 * h, 1024 * tcB + 512 * (h + 1))
                    p.op("T", lambda t, gsl=gsl, hsl=hsl, h=h: t.matmul(
                        psA[:, 512 * h:512 * (h + 1)], patd_sb[:, gsl], dt_sb[:, hsl],
                        start=True, stop=True),
                        reads=dt_r + ("patd_sb",), writes=(f"psA{h}",))
                    p.op("T", lambda t, gsl=gsl, hsl=hsl, h=h: t.matmul(
                        psU[:, 512 * h:512 * (h + 1)], patd_sb[:, gsl], u_sb[:, hsl],
                        start=True, stop=True),
                        reads=("u_sb", "patd_sb"), writes=(f"psU{h}",))
                if grp > 0:
                    emit_red(grp - 1)
                gb = grp % 2
                p.op("S", lambda s, dsub=dsub, grp=grp, gb=gb: s.activation(dA[gb][:, :], psA[:, :], AF.Exp,
                     scale=aimg[:, 16 * dsub + grp:16 * dsub + grp + 1]),
                     reads=("psA0", "psA1"), sreads=("aimg",), writes=(f"dA{gb}",))
                p.op("V", lambda v, csl=csl, gb=gb: v.tensor_mul(dBx[gb][:, :], psU[:, :], brep[:, csl]),
                     reads=("psU0", "psU1", "brep"), writes=(f"dBx{gb}",))
                p.op("V", lambda v, grp=grp, gb=gb: v.tensor_tensor_scan(h_sb[gb][:, :], dA[gb][:, :], dBx[gb][:, :],
                     state[:, grp:grp + 1], ALU.mult, ALU.add),
                     reads=(f"dA{gb}", f"dBx{gb}"), sreads=("state",), writes=(f"h_sb{gb}",))
                p.op("V", lambda v, grp=grp, gb=gb: v.tensor_copy(state[:, grp:grp + 1], h_sb[gb][:, 1023:1024]),
                     reads=(f"h_sb{gb}",), writes=("state",))
                p.op("G", lambda g, csl=csl, gb=gb: g.tensor_mul(hc[gb][:, :], h_sb[gb][:, :], crep[:, csl]),
                     reads=(f"h_sb{gb}", "crep"), writes=(f"hc{gb}",))
            emit_red(15)
            p.op("V", lambda v: v.tensor_copy(ybf[:, :], psY[:, :]),
                 reads=("psY0", "psY1"), writes=("ybf",))
            p.dma("Y", "yo", yT[512 * tcB + 128 * dsub:512 * tcB + 128 * (dsub + 1), :],
                  ybf[:, :], reads=("ybf",), out_sem=True)
    p.scope_close()


def build_ssm_post():
    p = Prog("ssm_post")
    io = dict(x=p.din("x", [DM, TOK], F32),
              xln=p.din("xln", [DM, TOK], F32),
              yT=p.din("yT", [DM, TOK], BF16),
              dimg=p.din("dimg", [128, NKT], F32),
              xo=p.dout("xo", [DM, TOK], F32))
    emit_ssm_post(p, io)
    return p


def emit_ssm_post(p, io):
    p.scope_open()
    x, xln, yT, dimg, xo = io["x"], io["xln"], io["yT"], io["dimg"], io["xo"]
    d_sb = p.sb("d_sb", [128, NKT], F32)
    xr = [p.sb(f"xr{b}", [128, TOK], F32) for b in range(2)]
    xl = [p.sb(f"xl{b}", [128, TOK], F32) for b in range(2)]
    yl = [p.sb(f"yl{b}", [128, TOK], BF16) for b in range(2)]
    p.dma("Y", "cd", d_sb[:, :], dimg, writes=("d_sb",))
    for k in range(NKT):
        b = k % 2
        p.dma("Y", f"x{b}", xr[b][:, :], x[128 * k:128 * (k + 1), :], writes=(f"xr{b}",))
        p.dma("Y", f"l{b}", xl[b][:, :], xln[128 * k:128 * (k + 1), :], writes=(f"xl{b}",))
        p.dma("Y", f"y{b}", yl[b][:, :], yT[128 * k:128 * (k + 1), :], writes=(f"yl{b}",))
        p.op("V", lambda v, b=b, k=k: v.scalar_tensor_tensor(
            xl[b][:, :], xl[b][:, :], d_sb[:, k:k + 1], yl[b][:, :], ALU.mult, ALU.add),
            reads=(f"xl{b}", f"yl{b}"), sreads=("d_sb",), writes=(f"t{b}",))
        p.op("V", lambda v, b=b: v.tensor_add(xr[b][:, :], xr[b][:, :], xl[b][:, :]),
             reads=(f"xr{b}", f"t{b}"), writes=(f"xr{b}",))
        p.dma("Y", f"o{b}", xo[128 * k:128 * (k + 1), :], xr[b][:, :],
              reads=(f"xr{b}",), out_sem=True)
    p.scope_close()


def host_ssm_consts(A_log, D, x_proj_w, dt_proj_w, dt_proj_b):
    xpw = np.zeros((DM, 256), dtype=np.float32)
    xpw[:, 0:160] = x_proj_w.T
    pre = dict(wimg=blocked_img(xpw),
               onescol=np.ones((128, 1), dtype=BF),
               onesrow=np.ones((1, 128), dtype=np.float32))
    patd = np.zeros((128, 16 * 128), dtype=BF)
    paty = np.zeros((128, 16 * 128), dtype=BF)
    patb = np.zeros((16, 128), dtype=BF)
    for g in range(16):
        for pp in range(128):
            patd[8 * g + pp // 16, 128 * g + pp] = 1
            paty[pp, 128 * g + 8 * g + pp // 16] = 1
    for pp in range(128):
        patb[pp % 16, pp] = 1
    cores = []
    for c in range(NCORE):
        d0 = 512 * c
        alog = np.zeros((128, 64), dtype=np.float32)
        for dsub in range(4):
            for g in range(16):
                for pp in range(128):
                    alog[pp, 16 * dsub + g] = A_log[d0 + 128 * dsub + 8 * g + pp // 16, pp % 16]
        dtw = np.zeros((128, 512), dtype=np.float32)
        for dsub in range(4):
            dtw[:, 128 * dsub:128 * (dsub + 1)] = dt_proj_w[d0 + 128 * dsub:d0 + 128 * (dsub + 1), :].T
        dtb = dt_proj_b[d0:d0 + 512].reshape(4, 128).T.astype(np.float32)
        cores.append(dict(alog=alog, dtw=dtw, dtb=np.ascontiguousarray(dtb),
                          patd=patd, patb=patb, paty=paty))
    post = dict(dimg=np.ascontiguousarray(D.reshape(NKT, 128).T).astype(np.float32))
    return pre, cores, post


def run_ssm(xT_shards, consts3):
    pre_c, cores_c, post_c = consts3
    nc1 = get_prog("ssm_pre", build_ssm_pre)
    res1 = run_bass_kernel_spmd(nc1, [dict(pre_c, x=np.ascontiguousarray(s)) for s in xT_shards],
                                core_ids=list(range(NCORE)))
    xln = [np.asarray(r["xlno"]) for r in res1.results]
    xpf = np.concatenate([np.asarray(r["xpo"]) for r in res1.results], axis=0)
    nc2 = get_prog("ssm_core", build_ssm_core)
    in2 = []
    for c in range(NCORE):
        xslc = np.concatenate([xln[i][512 * c:512 * (c + 1)] for i in range(NCORE)], axis=0)
        in2.append(dict(cores_c[c], xslc=np.ascontiguousarray(xslc), xpf=np.ascontiguousarray(xpf)))
    res2 = run_bass_kernel_spmd(nc2, in2, core_ids=list(range(NCORE)))
    yT = [np.asarray(r["yT"]) for r in res2.results]
    nc3 = get_prog("ssm_post", build_ssm_post)
    in3 = []
    for c in range(NCORE):
        ym = np.concatenate([yT[i][512 * c:512 * (c + 1)] for i in range(NCORE)], axis=0)
        in3.append(dict(post_c, x=np.ascontiguousarray(xT_shards[c]),
                        xln=np.ascontiguousarray(xln[c]), yT=np.ascontiguousarray(ym)))
    res3 = run_bass_kernel_spmd(nc3, in3, core_ids=list(range(NCORE)))
    return [r["xo"] for r in res3.results]




def build_dequant():
    p = Prog("dequant")
    io = dict(codes=p.din("codes", [DM, 1280], I32),
              scales=p.din("scales", [1, 32 * 1280], F32),
              onesrow=p.din("onesrow", [1, 128], F32),
              wcat=p.dout("wcat", [DM, 1280], BF16))
    emit_dequant(p, io)
    return p


def emit_dequant(p, io, scoped=True):
    if scoped:
        p.scope_open()
    codes, scales, ones_row_d, wcat = io["codes"], io["scales"], io["onesrow"], io["wcat"]
    ct = [p.sb(f"ct{b}", [128, 512], I32) for b in range(2)]
    cf = [p.sb(f"cf{b}", [128, 512], F32) for b in range(2)]
    wb = [p.sb(f"wb{b}", [128, 512], BF16) for b in range(2)]
    sc = p.sb("sc", [1, 1280], F32)
    onr = p.sb("onr", [1, 128], F32)
    ps = [p.psum(f"ps{i}", [128, 512], F32) for i in range(2)]
    p.dma("Y", "co", onr[:, :], ones_row_d, writes=("onr",))
    chunks = [(0, 512), (512, 512), (1024, 256)]
    i = 0
    for g in range(32):
        p.dma("Y", "cs", sc[:, :], scales[0:1, 1280 * g:1280 * (g + 1)], writes=("sc",))
        for off, w in chunks:
            b = i % 2
            p.dma("Y", f"c{b}", ct[b][:, 0:w], codes[128 * g:128 * (g + 1), off:off + w],
                  writes=(f"ct{b}",))
            p.op("V", lambda v, b=b, w=w: v.tensor_scalar(cf[b][:, 0:w], ct[b][:, 0:w],
                 8.0, 0.5, ALU.subtract, ALU.mult),
                 reads=(f"ct{b}",), writes=(f"cf{b}",))
            p.op("T", lambda t, b=b, off=off, w=w: t.matmul(ps[b][:, 0:w], onr[0:1, :],
                 sc[0:1, off:off + w], start=True, stop=True),
                 reads=("sc", "onr"), writes=(f"ps{b}",))
            p.op("V", lambda v, b=b, w=w: v.tensor_mul(wb[b][:, 0:w], cf[b][:, 0:w], ps[b][:, 0:w]),
                 reads=(f"cf{b}", f"ps{b}"), writes=(f"wb{b}",))
            p.dma("Y", f"o{b}", wcat[128 * g:128 * (g + 1), off:off + w], wb[b][:, 0:w],
                  reads=(f"wb{b}",), writes=(f"dq_st{b}",), out_sem=True)
            i += 1
    p.op("V", lambda v: v.engine_nop(), reads=("dq_st0", "dq_st1"), writes=("dq_all",))
    if scoped:
        p.scope_close()


def run_dequant(q_codes, k_codes, v_codes, o_codes, q_scales, k_scales, v_scales, o_scales):
    nc = get_prog("dequant", build_dequant)
    in_maps = []
    for c in range(NCORE):
        cc = np.concatenate([q_codes[:, 512 * c:512 * (c + 1)], o_codes[:, 512 * c:512 * (c + 1)],
                             k_codes[:, 128 * c:128 * (c + 1)], v_codes[:, 128 * c:128 * (c + 1)]],
                            axis=1).astype(np.int32)
        ss = np.concatenate([q_scales[:, 512 * c:512 * (c + 1)], o_scales[:, 512 * c:512 * (c + 1)],
                             k_scales[:, 128 * c:128 * (c + 1)], v_scales[:, 128 * c:128 * (c + 1)]],
                            axis=1).astype(np.float32)
        in_maps.append(dict(codes=np.ascontiguousarray(cc),
                            scales=np.ascontiguousarray(ss.reshape(1, -1)),
                            onesrow=np.ones((1, 128), dtype=np.float32)))
    res = run_bass_kernel_spmd(nc, in_maps, core_ids=list(range(NCORE)))
    wq = np.concatenate([np.asarray(r["wcat"])[:, 0:512] for r in res.results], axis=1)
    wo = np.concatenate([np.asarray(r["wcat"])[:, 512:1024] for r in res.results], axis=1)
    wk = np.concatenate([np.asarray(r["wcat"])[:, 1024:1152] for r in res.results], axis=1)
    wv = np.concatenate([np.asarray(r["wcat"])[:, 1152:1280] for r in res.results], axis=1)
    return wq, wk, wv, wo




def emit_attn_pre_wall(p, croot, io):
    p.scope_open()
    x, wall, qkvp = io["x"], io["wall"], io["qkvp"]
    c = stage_ln(p, croot)
    pb = c["pb"]
    slab = [p.sb(f"slab{b}", [128, 1280], BF16) for b in range(3)]
    ob = p.sb("ob", [128, 512], BF16)
    emit_ln(p, c, x)
    xln = c["xln"]
    cols = [(0, 128), (128, 128), (256, 128), (384, 128), (1024, 128), (1152, 128)]
    rowbase = [0, 128, 256, 384, 512, 640]
    for j in range(NCORE):
        for h in range(2):
            for kt in range(NKT):
                b = kt % 3
                p.dma("Y" if b == 0 else "G", f"sl{b}", slab[b][:, :],
                      wall[DM * j + 128 * kt:DM * j + 128 * (kt + 1), :],
                      reads=("wall_r",), writes=(f"slab{b}",))
                for mi, (off, wdt) in enumerate(cols):
                    p.op("T", lambda t, b=b, mi=mi, off=off, h=h, kt=kt: t.matmul(
                        pb[mi][:, :], slab[b][:, off:off + 128],
                        xln[:, 1024 * kt + 512 * h:1024 * kt + 512 * (h + 1)],
                        start=(kt == 0), stop=(kt == NKT - 1)),
                        reads=(f"slab{b}",) + xln_reads(kt), writes=(f"pb{mi}",))
            for mi in range(6):
                p.op("S", lambda s, mi=mi: s.copy(ob[:, :], pb[mi][:, :]),
                     reads=(f"pb{mi}",), writes=("ob",))
                p.dma("Y", "o0", qkvp[768 * j + rowbase[mi]:768 * j + rowbase[mi] + 128,
                      512 * h:512 * (h + 1)], ob[:, :], reads=("ob",), out_sem=True)
    p.scope_close()


def emit_attn_post_wall(p, io):
    p.scope_open()
    x, at, wall, xo = io["x"], io["at"], io["wall"], io["xo"]
    slab = [p.sb(f"slab{b}", [128, 512], BF16) for b in range(3)]
    att = p.sb("att", [128, NKT * TOK], BF16)
    xr = [p.sb(f"xr{b}", [128, TOK], F32) for b in range(2)]
    pb = [p.psum(f"pb{i}", [128, 512], F32) for i in range(8)]
    scale_tile = emit_post_rcp_prologue(p, io, att, pb)
    for k in range(NKT):
        p.dma("Y", f"a{k % 2}", att[:, 1024 * k:1024 * (k + 1)], at[128 * k:128 * (k + 1), :],
              writes=(f"att{k}",))
        scale_tile(k)
    for j in range(NCORE):
        for kt in range(NKT):
            b = kt % 3
            p.dma("Y" if b == 0 else "G", f"sl{b}", slab[b][:, :],
                  wall[DM * j + 128 * kt:DM * j + 128 * (kt + 1), 512:1024],
                  reads=("wall_r",), writes=(f"slab{b}",))
            for s in range(4):
                for h in range(2):
                    p.op("T", lambda t, b=b, s=s, h=h, kt=kt: t.matmul(
                        pb[2 * s + h][:, :], slab[b][:, 128 * s:128 * (s + 1)],
                        att[:, 1024 * kt + 512 * h:1024 * kt + 512 * (h + 1)],
                        start=(kt == 0), stop=(kt == NKT - 1)),
                        reads=(f"slab{b}", f"att{kt}"), writes=(f"pb{2 * s + h}",))
        for s in range(4):
            m = 4 * j + s
            bb = s % 2
            p.dma("Y", f"xr{bb}", xr[bb][:, :], x[128 * m:128 * (m + 1), :], writes=(f"xr{bb}",))
            for h in range(2):
                p.op("V", lambda v, bb=bb, s=s, h=h: v.tensor_add(
                    xr[bb][:, 512 * h:512 * (h + 1)], xr[bb][:, 512 * h:512 * (h + 1)],
                    pb[2 * s + h][:, :]),
                    reads=(f"xr{bb}", f"pb{2 * s + h}"), writes=(f"xr{bb}",))
            p.dma("Y", f"xo{bb}", xo[128 * m:128 * (m + 1), :], xr[bb][:, :],
                  reads=(f"xr{bb}",), out_sem=True)
    p.scope_close()


def build_fused(upto=99):
    p = Prog("mk_fused" + ("" if upto >= 99 else f"_u{upto}"))
    nc = p.nc
    oc_d = p.din("onescol", [128, 1], BF16)
    orw_d = p.din("onesrow", [1, 128], F32)
    trimask_d = p.din("trimask", [4, 128, 512], BF16)
    croot = alloc_ln_root(p)
    emit_ln_consts(p, croot, oc_d, orw_d)

    x0 = p.din("x0", [DM, TOK], F32)
    xo_ext = p.dout("xout", [DM, TOK], F32)
    xb = [nc.dram_tensor(f"xb{i}", [DM, TOK], F32) for i in range(3)]

    def attn_block(sfx, x_in, x_out, wdt):
        pre_w = p.din("wpre" + sfx, [48, 128, NKT * 128], F32 if wdt == "f32" else BF16) \
            if wdt != "dram" else None
        post_w = p.din("wpost" + sfx, [NKT, 128, NKT * 128], F32 if wdt == "f32" else BF16) \
            if wdt != "dram" else None
        return pre_w, post_w

    def run_attn_block(sfx, x_in, x_out, pre_w_ap, post_w_ap, cos_ap, sin_ap):
        qkvp = nc.dram_tensor("qkvp" + sfx, [6144, TOK], BF16)
        qkvm = nc.dram_tensor("qkvm" + sfx, [6144, TOK], BF16)
        aout = nc.dram_tensor("aout" + sfx, [DM, TOK], BF16)
        at = nc.dram_tensor("at" + sfx, [DM, TOK], BF16)
        dend = nc.dram_tensor("dend" + sfx, [64, 512], F32)
        dend_all = nc.dram_tensor("dend_all" + sfx, [64, 512], F32)
        dend_rcp = nc.dram_tensor("dend_rcp" + sfx, [NKT, TOK], F32)
        emit_attn_pre(p, croot, dict(x=x_in, wimg=pre_w_ap, qkvp=qkvp.ap()))
        p.cc("AllToAll", ins=[qkvp.ap().opt()], outs=[qkvm.ap().opt()])
        p.barrier()
        emit_attn_core(p, dict(qkvm=qkvm.ap(), cosf=cos_ap, sinf=sin_ap,
                               trimask=trimask_d, onescol=oc_d, onesrow=orw_d,
                               aout=aout.ap(), dend=dend.ap()))
        p.cc("AllToAll", ins=[aout.ap().opt()], outs=[at.ap().opt()])
        p.cc("AllToAll", ins=[dend.ap().opt()], outs=[dend_all.ap().opt()])
        p.barrier()
        emit_attn_post(p, dict(x=x_in, at=at.ap(), wimg=post_w_ap, xo=x_out,
                               dA=dend_all.ap(), dAr=dend_rcp.ap(), onesrow=orw_d))

    wcat = nc.dram_tensor("wcat", [DM, 1280], BF16)
    wall = nc.dram_tensor("wall", [NCORE * DM, 1280], BF16, addr_space="Shared")
    def finish_early(cur):
        p.scope_open()
        fb = [p.sb(f"fb{b}", [128, TOK], F32) for b in range(2)]
        for k in range(NKT):
            b = k % 2
            p.dma("Y", f"fl{b}", fb[b][:, :], cur[128 * k:128 * (k + 1), :], writes=(f"fb{b}",))
            p.dma("Y", f"fs{b}", xo_ext[128 * k:128 * (k + 1), :], fb[b][:, :],
                  reads=(f"fb{b}",), out_sem=True)
        p.scope_close()
        return p

    if upto <= -3:
        return finish_early(x0)
    emit_dequant(p, dict(codes=p.din("codes", [DM, 1280], I32),
                         scales=p.din("scales", [1, 32 * 1280], F32),
                         onesrow=orw_d, wcat=wcat.ap()), scoped=True)
    if upto <= -2:
        return finish_early(x0)
    p.cc("AllGather", ins=[wcat.ap().opt()], outs=[wall.ap().opt()], reads=("dq_all",),
         writes=("wall_r",))
    if upto <= -1:
        return finish_early(x0)
    wall_ap = wall.ap()



    if upto <= 0:
        return finish_early(x0)
    run_attn_block("a0", x0, xb[0].ap(),
                   p.din("wpre_a0", [48, 128, NKT * 128], F32),
                   p.din("wpost_a0", [NKT, 128, NKT * 128], F32),
                   p.din("cos0", [128, SEQ], F32), p.din("sin0", [128, SEQ], F32))
    if upto <= 1:
        return finish_early(xb[0].ap())
    iof0 = ffn_io(p, "_f0", with_x=False)
    iof0["x"] = xb[0].ap(); iof0["xo"] = xb[1].ap()
    emit_ffn(p, croot, iof0, "l0")
    if upto <= 2:
        return finish_early(xb[1].ap())
    xlno = nc.dram_tensor("xlno", [DM, TOK], F32)
    xpo = nc.dram_tensor("xpo", [160, TOK], F32)
    xslc = nc.dram_tensor("xslc", [DM, TOK], F32)
    xpf = nc.dram_tensor("xpf", [1280, TOK], F32, addr_space="Shared")
    yTb = nc.dram_tensor("yTb", [DM, TOK], BF16)
    ymb = nc.dram_tensor("ymb", [DM, TOK], BF16)
    emit_ssm_pre(p, croot, dict(x=xb[1].ap(), wimg=p.din("wimg_sp", [2, 128, NKT * 128], F32),
                                xlno=xlno.ap(), xpo=xpo.ap()))
    p.cc("AllToAll", ins=[xlno.ap().opt()], outs=[xslc.ap().opt()])
    p.cc("AllGather", ins=[xpo.ap().opt()], outs=[xpf.ap().opt()])
    p.barrier()
    emit_ssm_core(p, dict(xslc=xslc.ap(), xpf=xpf.ap(),
                          alog=p.din("alog", [128, 64], F32),
                          dtw=p.din("dtw", [128, 512], F32),
                          dtb=p.din("dtb", [128, 4], F32),
                          patd=p.din("patd", [128, 16 * 128], BF16),
                          patb=p.din("patb", [16, 128], BF16),
                          paty=p.din("paty", [128, 16 * 128], BF16),
                          yT=yTb.ap()))
    p.cc("AllToAll", ins=[yTb.ap().opt()], outs=[ymb.ap().opt()])
    p.barrier()
    emit_ssm_post(p, dict(x=xb[1].ap(), xln=xlno.ap(), yT=ymb.ap(),
                          dimg=p.din("dimg", [128, NKT], F32), xo=xb[2].ap()))
    if upto <= 3:
        return finish_early(xb[2].ap())
    iof0b = dict(iof0)
    iof0b["x"] = xb[2].ap(); iof0b["xo"] = xb[0].ap()
    emit_ffn(p, croot, iof0b, "l0")
    if upto <= 4:
        return finish_early(xb[0].ap())
    run_attn_block("a1", xb[0].ap(), xb[1].ap(),
                   p.din("wpre_a1", [48, 128, NKT * 128], F32),
                   p.din("wpost_a1", [NKT, 128, NKT * 128], F32),
                   p.din("cos1", [128, SEQ], F32), p.din("sin1", [128, SEQ], F32))
    if upto <= 5:
        return finish_early(xb[1].ap())
    iof1 = ffn_io(p, "_f1", with_x=False)
    iof1["x"] = xb[1].ap(); iof1["xo"] = xb[2].ap()
    emit_ffn(p, croot, iof1, "l1")
    if upto <= 6:
        return finish_early(xb[2].ap())
    qkvp_ab = nc.dram_tensor("qkvp_ab", [6144, TOK], BF16)
    qkvm_ab = nc.dram_tensor("qkvm_ab", [6144, TOK], BF16)
    aout_ab = nc.dram_tensor("aout_ab", [DM, TOK], BF16)
    at_ab = nc.dram_tensor("at_ab", [DM, TOK], BF16)
    dend_ab = nc.dram_tensor("dend_ab", [64, 512], F32)
    dend_all_ab = nc.dram_tensor("dend_all_ab", [64, 512], F32)
    dend_rcp_ab = nc.dram_tensor("dend_rcp_ab", [NKT, TOK], F32)
    emit_attn_pre_wall(p, croot, dict(x=xb[2].ap(), wall=wall_ap, qkvp=qkvp_ab.ap()))
    p.cc("AllToAll", ins=[qkvp_ab.ap().opt()], outs=[qkvm_ab.ap().opt()])
    p.barrier()
    emit_attn_core(p, dict(qkvm=qkvm_ab.ap(), cosf=p.din("cos1b", [128, SEQ], F32),
                           sinf=p.din("sin1b", [128, SEQ], F32),
                           trimask=trimask_d, onescol=oc_d, onesrow=orw_d,
                           aout=aout_ab.ap(), dend=dend_ab.ap()))
    p.cc("AllToAll", ins=[aout_ab.ap().opt()], outs=[at_ab.ap().opt()])
    p.cc("AllToAll", ins=[dend_ab.ap().opt()], outs=[dend_all_ab.ap().opt()])
    p.barrier()
    emit_attn_post_wall(p, dict(x=xb[2].ap(), at=at_ab.ap(), wall=wall_ap, xo=xb[0].ap(),
                                dA=dend_all_ab.ap(), dAr=dend_rcp_ab.ap(), onesrow=orw_d))
    if upto <= 7:
        return finish_early(xb[0].ap())
    iof1b = dict(iof1)
    iof1b["x"] = xb[0].ap(); iof1b["xo"] = xo_ext
    emit_ffn(p, croot, iof1b, "l1")
    return p


def kernel(**inputs):
    x = np.asarray(inputs["L0_attn_a_x"])[0]
    c_a0 = host_attn_consts("l0", inputs["L0_attn_a_Wq"], inputs["L0_attn_a_Wk"],
                            inputs["L0_attn_a_Wv"], inputs["L0_attn_a_Wo"])
    c_a1 = host_attn_consts("l1", inputs["L1_attn_a_Wq"], inputs["L1_attn_a_Wk"],
                            inputs["L1_attn_a_Wv"], inputs["L1_attn_a_Wo"])
    c_f0 = host_ffn_consts("l0", inputs["L0_ffn_W_base"], inputs["L0_ffn_b_base"],
                           inputs["L0_ffn_W_router"], None,
                           inputs["L0_ffn_lora_A"], inputs["L0_ffn_lora_B"])
    c_f1 = host_ffn_consts("l1", inputs["L1_ffn_W_base"], inputs["L1_ffn_b_base"],
                           inputs["L1_ffn_W_router"], inputs["L1_ffn_b_router"],
                           inputs["L1_ffn_lora_A"], inputs["L1_ffn_lora_B"])
    c_s_pre, c_s_cores, c_s_post = host_ssm_consts(
        inputs["L0_attn_b_A_log"], inputs["L0_attn_b_D"], inputs["L0_attn_b_x_proj_w"],
        inputs["L0_attn_b_dt_proj_w"], inputs["L0_attn_b_dt_proj_b"])
    cos1b, sin1b = host_rope_tables("l1")

    nc = get_prog("fused", lambda: build_fused())

    def ffn_maps(c, sfx):
        return {"wimg" + sfx: c["wimg"], "bbi" + sfx: c["bbi"], "wr" + sfx: c["wr"],
                "br" + sfx: c["br"], "aimg" + sfx: c["aimg"], "bstk" + sfx: c["bstk"],
                "patw" + sfx: c["patw"], "idf" + sfx: c["idf"]}

    in_maps = []
    for c in range(NCORE):
        cc_codes = np.concatenate(
            [inputs["L1_attn_b_q_codes"][:, 512 * c:512 * (c + 1)],
             inputs["L1_attn_b_o_codes"][:, 512 * c:512 * (c + 1)],
             inputs["L1_attn_b_k_codes"][:, 128 * c:128 * (c + 1)],
             inputs["L1_attn_b_v_codes"][:, 128 * c:128 * (c + 1)]], axis=1).astype(np.int32)
        cc_scales = np.concatenate(
            [inputs["L1_attn_b_q_scales"][:, 512 * c:512 * (c + 1)],
             inputs["L1_attn_b_o_scales"][:, 512 * c:512 * (c + 1)],
             inputs["L1_attn_b_k_scales"][:, 128 * c:128 * (c + 1)],
             inputs["L1_attn_b_v_scales"][:, 128 * c:128 * (c + 1)]], axis=1).astype(np.float32)
        m = dict(
            onescol=np.ones((128, 1), dtype=BF),
            onesrow=np.ones((1, 128), dtype=np.float32),
            trimask=host_trimask(),
            x0=np.ascontiguousarray(x[1024 * c:1024 * (c + 1)].T),
            codes=np.ascontiguousarray(cc_codes),
            scales=np.ascontiguousarray(cc_scales.reshape(1, -1)),
            wpre_a0=c_a0[0]["wimg"], wpost_a0=c_a0[2]["wimg"],
            cos0=c_a0[1]["cosf"], sin0=c_a0[1]["sinf"],
            wpre_a1=c_a1[0]["wimg"], wpost_a1=c_a1[2]["wimg"],
            cos1=c_a1[1]["cosf"], sin1=c_a1[1]["sinf"],
            cos1b=cos1b, sin1b=sin1b,
            wimg_sp=c_s_pre["wimg"], dimg=c_s_post["dimg"],
            **c_s_cores[c],
            **ffn_maps(c_f0, "_f0"), **ffn_maps(c_f1, "_f1"),
        )
        in_maps.append(m)
    res = run_bass_kernel_spmd(nc, in_maps, core_ids=list(range(NCORE)))
    out = np.concatenate([np.asarray(r["xout"]).T for r in res.results], axis=0)
    return out[None].astype(np.float32)


# version 21
# speedup vs baseline: 1.1540x; 1.1540x over previous
import bisect
import numpy as np
import ml_dtypes
from contextlib import ExitStack
import concourse.bass as bass
import concourse.mybir as mybir
from concourse.bass_utils import run_bass_kernel_spmd
from concourse.aot_env import allow_ici_all_to_all

F32 = mybir.dt.float32
BF16 = mybir.dt.bfloat16
I32 = mybir.dt.int32
AF = mybir.ActivationFunctionType
ALU = mybir.AluOpType
AX = mybir.AxisListType

BF = ml_dtypes.bfloat16
NCORE = 8
TOK = 1024
DM = 4096
SEQ = 8192
NKT = DM // 128
EPS = 1e-5

ENGINES = ("T", "V", "S", "G", "Y")


class Prog:
    def __init__(self, name):
        self.nc = bass.Bass(target_bir_lowering=False, name=name)
        self.stream = {e: [] for e in ENGINES}
        self.cnt = {e: 0 for e in ENGINES}
        self.waited = {e: set() for e in ENGINES}
        self.dcnt = {}
        self.max_waited = {}
        self.ccnt = 0
        self.res = {}
        self.last_e = {e: 0 for e in ENGINES}
        self._ctx = ExitStack()
        self._sid = 0
        self._scope = None
        self._pending_bar = None
        self._esync = {}
        self._bar_tokens = None
        self.out_sems = set()

    def _r(self, name):
        return self.res.setdefault(name, {"w": None, "rs": []})

    def _gather(self, eng, reads, writes, sreads=()):
        waits = {}

        def add(tok, force=False):
            if tok is None:
                return
            if tok[0] == "E":
                _, e, seq = tok
                if e == eng and not force and eng == "T":
                    return
                self.waited[e].add(seq)
                key, val = "E" + e, seq
            elif tok[0] == "D":
                _, sem, k = tok
                val = self.dcnt[sem]
                self.max_waited[sem] = max(self.max_waited.get(sem, 0), val)
                key, tok = sem, ("D", sem, val)
            else:
                key, val = "CC", tok[1]
            old = waits.get(key)
            if old is None or tok[-1] > old[-1]:
                waits[key] = tok

        for r in reads:
            add(self._r(r)["w"])
        for r in sreads:
            add(self._r(r)["w"], force=True)
        for w in writes:
            rr = self._r(w)
            add(rr["w"])
            for t in rr["rs"]:
                add(t)
        return waits

    def _apply_barrier(self, eng, waits):
        if self._pending_bar and eng in self._pending_bar:
            self._pending_bar.discard(eng)
            self._esync[eng] = True
            for tok in self._bar_tokens:
                if tok[0] == "E":
                    self.waited[tok[1]].add(tok[2])
                    key = "E" + tok[1]
                elif tok[0] == "D":
                    key = tok[1]
                else:
                    key = "CC"
                old = waits.get(key)
                if old is None or tok[-1] > old[-1]:
                    waits[key] = tok

    def _commit(self, eng, waits, fn, mytok, reads, writes):
        for r in reads:
            self._r(r)["rs"].append(mytok)
        for w in writes:
            self.res[w] = {"w": mytok, "rs": []}
        self.stream[eng].append((list(waits.items()), fn, mytok))

    def op(self, eng, fn, reads=(), writes=(), sreads=()):
        waits = self._gather(eng, reads, writes, sreads)
        self._apply_barrier(eng, waits)
        self.cnt[eng] += 1
        self.last_e[eng] = self.cnt[eng]
        tok = ("E", eng, self.cnt[eng])
        self._commit(eng, waits, fn, tok, tuple(reads) + tuple(sreads), writes)
        return tok

    def dma(self, eng, sem, out, in_, reads=(), writes=(), out_sem=False, **kw):
        waits = self._gather(eng, reads, writes)
        self._apply_barrier(eng, waits)
        if self.max_waited.get(sem, 0) >= self.dcnt.get(sem, 0) and self.dcnt.get(sem, 0) > 0:
            if not waits and not self._esync.get(eng):
                raise AssertionError(f"dma on frozen sem {sem} without ordering deps")
        self.dcnt[sem] = self.dcnt.get(sem, 0) + 1
        tok = ("D", sem, self.dcnt[sem])
        if out_sem:
            self.out_sems.add(sem)
        self.cnt[eng] += 1
        self._commit(eng, waits,
                     (lambda e, out=out, in_=in_, kw=kw: e.dma_start(out=out, in_=in_, **kw)),
                     tok, reads, writes)
        return tok

    def cc(self, kind, ins, outs, reads=(), writes=(), op=None):
        eng = "G"
        waits = self._gather(eng, reads, writes)
        self._apply_barrier(eng, waits)
        if self.ccnt > 0:
            waits["CC"] = ("C", self.ccnt)
        self.ccnt += 1
        tok = ("C", self.ccnt)
        alu = op if op is not None else ALU.bypass

        def fn(e, kind=kind, alu=alu, ins=ins, outs=outs):
            with allow_ici_all_to_all():
                return e.collective_compute(kind, alu, replica_groups=[list(range(NCORE))],
                                            ins=ins, outs=outs)
        self.cnt[eng] += 1
        self._commit(eng, waits, fn, tok, reads, writes)
        return tok

    def sb(self, name, shape, dtype):
        ctx = self._scope if self._scope is not None else self._ctx
        return ctx.enter_context(self.nc.sbuf_tensor(f"s{self._sid}_{name}", shape, dtype))

    def psum(self, name, shape, dtype=F32):
        ctx = self._scope if self._scope is not None else self._ctx
        return ctx.enter_context(self.nc.psum_tensor(f"s{self._sid}_{name}", shape, dtype))

    def scope_open(self):
        assert self._scope is None
        self._sid += 1
        self._scope = ExitStack()

    def scope_close(self):
        self.barrier()
        self._scope.close()
        self._scope = None

    def barrier(self):
        toks = []
        for e in ENGINES:
            if self.last_e[e]:
                toks.append(("E", e, self.last_e[e]))
        for s, k in self.dcnt.items():
            toks.append(("D", s, k))
            self.max_waited[s] = k
        if self.ccnt:
            toks.append(("C", self.ccnt))
        self._bar_tokens = toks
        self._pending_bar = set(ENGINES)
        self._esync = {}

    def din(self, name, shape, dtype):
        return self.nc.dram_tensor(name, shape, dtype, kind="ExternalInput").ap()

    def dout(self, name, shape, dtype):
        return self.nc.dram_tensor(name, shape, dtype, kind="ExternalOutput").ap()

    def dint(self, name, shape, dtype, shared=False):
        if shared:
            return self.nc.dram_tensor(name, shape, dtype, addr_space="Shared").ap()
        return self.nc.dram_tensor(name, shape, dtype).ap()

    def finish(self):
        nc = self.nc
        sems = {}
        with self._ctx:
            for e in ENGINES:
                if self.waited[e]:
                    sems["E" + e] = self._ctx.enter_context(nc.semaphore("q" + e))
            for s in self.dcnt:
                sems[s] = self._ctx.enter_context(nc.semaphore("d_" + s))
            if self.ccnt:
                sems["CC"] = self._ctx.enter_context(nc.semaphore("d_cc"))
            block = self._ctx.enter_context(nc.Block())
            worder = {e: sorted(self.waited[e]) for e in ENGINES}

            def wval(tok):
                if tok[0] == "E":
                    return bisect.bisect_right(worder[tok[1]], tok[2])
                if tok[0] == "D":
                    return tok[2] * 16
                return tok[1]

            def make_run(key):
                def run(eng):
                    for waits, fn, tok in self.stream[key]:
                        for sem_key, wt in waits:
                            eng.wait_ge(sems[sem_key], wval(wt))
                        inst = fn(eng)
                        if tok[0] == "E":
                            if tok[2] in self.waited[key]:
                                inst.then_inc(sems["E" + key], 1)
                        elif tok[0] == "D":
                            inst.then_inc(sems[tok[1]], 16)
                        else:
                            inst.then_inc(sems["CC"], 1)
                    if key == "Y":
                        for s in sorted(self.out_sems):
                            eng.wait_ge(sems[s], self.dcnt[s] * 16)
                        if self.ccnt:
                            eng.wait_ge(sems["CC"], self.ccnt)
                return run

            block.tensor(make_run("T"))
            block.vector(make_run("V"))
            block.scalar(make_run("S"))
            block.gpsimd(make_run("G"))
            block.sync(make_run("Y"))
        return nc


def blocked_img(mat):
    K, M = mat.shape
    kt, mb = K // 128, M // 128
    return np.ascontiguousarray(
        mat.reshape(kt, 128, mb, 128).transpose(2, 1, 0, 3).reshape(mb, 128, kt * 128))


def small_img(mat):
    K, M = mat.shape
    kt = K // 128
    return np.ascontiguousarray(mat.reshape(kt, 128, M).transpose(1, 0, 2).reshape(128, kt * M))


def alloc_ln_root(p):
    c = {}
    c["xf"] = [p.sb(f"ln_xf{b}", [128, TOK], F32) for b in range(2)]
    c["sq"] = [p.sb(f"ln_sq{b}", [128, TOK], BF16) for b in range(2)]
    c["xc"] = [p.sb(f"ln_xc{b}", [128, TOK], BF16) for b in range(2)]
    c["t1"] = [p.sb(f"ln_t1{b}", [128, TOK], F32) for b in range(2)]
    c["m_row"] = p.sb("m_row", [1, TOK], F32)
    c["v_row"] = p.sb("v_row", [1, TOK], F32)
    c["rs_row"] = p.sb("rs_row", [1, TOK], F32)
    c["tmp_row"] = p.sb("tmp_row", [1, TOK], F32)
    c["ones_col"] = p.sb("ones_col", [128, 1], BF16)
    c["ones_row_f"] = p.sb("ones_row_f", [1, 128], F32)
    c["st_xb"] = [p.sb(f"st_xb{b}", [128, 512], BF16) for b in range(2)]
    c["st_sq"] = [p.sb(f"st_sq{b}", [128, 512], BF16) for b in range(2)]
    return c


def stage_ln(p, croot, want_f32=False):
    c = dict(croot)
    c["xln"] = p.sb("xln", [128, NKT * TOK], BF16)
    if want_f32:
        c["xlnf"] = [p.sb(f"xlnf{b}", [128, TOK], F32) for b in range(2)]
    c["pb"] = [p.psum(f"pb{i}", [128, 512], F32) for i in range(8)]
    return c


def alloc_ln(p, want_f32=False):
    c = stage_ln(p, alloc_ln_root(p), want_f32)
    return c


def emit_ln_consts(p, c, ones_col_d, ones_row_d):
    p.dma("Y", "cns", c["ones_col"][:, :], ones_col_d, writes=("ones_col",))
    p.dma("Y", "cns", c["ones_row_f"][:, :], ones_row_d, writes=("ones_row",))


def emit_stats_tile(p, croot, pbs, psn, src, src_res, m, h):
    b = h
    xb, sqb = croot["st_xb"][b], croot["st_sq"][b]
    p.op("V", lambda v, xb=xb, src=src: v.tensor_copy(xb[:, :], src),
         reads=(src_res,), writes=(f"stxb{b}",))
    p.op("S", lambda s, sqb=sqb, src=src: s.activation(sqb[:, :], src, AF.Square),
         reads=(src_res,), writes=(f"stsq{b}",))
    p.op("T", lambda t, xb=xb, h=h, m=m: t.matmul(
        pbs[h][0:1, :], croot["ones_col"][:, 0:1], xb[:, :],
        start=(m == 0), stop=(m == NKT - 1)),
        reads=(f"stxb{b}", "ones_col"), writes=(psn[h],))
    p.op("T", lambda t, sqb=sqb, h=h, m=m: t.matmul(
        pbs[2 + h][0:1, :], croot["ones_col"][:, 0:1], sqb[:, :],
        start=(m == 0), stop=(m == NKT - 1)),
        reads=(f"stsq{b}", "ones_col"), writes=(psn[2 + h],))


def emit_stats_fin(p, c, pbs, psn):
    mr, vr, rsr, tmp = c["m_row"], c["v_row"], c["rs_row"], c["tmp_row"]
    for h in range(2):
        p.op("V", lambda v, h=h: v.tensor_scalar(mr[:, 512 * h:512 * (h + 1)],
             pbs[h][0:1, :], 1.0 / DM, None, ALU.mult),
             reads=(psn[h],), writes=("m_row",))
        p.op("V", lambda v, h=h: v.tensor_scalar(vr[:, 512 * h:512 * (h + 1)],
             pbs[2 + h][0:1, :], 1.0 / DM, None, ALU.mult),
             reads=(psn[2 + h],), writes=("v_row",))
    p.op("V", lambda v: v.tensor_mul(tmp[:, :], mr[:, :], mr[:, :]),
         reads=("m_row",), writes=("tmp_row",))
    p.op("V", lambda v: v.tensor_sub(vr[:, :], vr[:, :], tmp[:, :]),
         reads=("v_row", "tmp_row"), writes=("v_row",))
    p.op("V", lambda v: v.tensor_scalar(vr[:, :], vr[:, :], float(EPS), None, ALU.add),
         reads=("v_row",), writes=("v_row",))
    p.op("S", lambda s: s.activation(vr[:, :], vr[:, :], AF.Sqrt),
         reads=("v_row",), writes=("v_row",))
    p.op("V", lambda v: v.reciprocal(rsr[:, :], vr[:, :]),
         reads=("v_row",), writes=("rs_row",))


def emit_ln(p, c, x_dram, want_f32=False, f32_sink=None, stats_ready=False):
    xf, sq, xc, pb = c["xf"], c["sq"], c["xc"], c["pb"]
    nb = len(xf)
    if not stats_ready:
        for k in range(NKT):
            b = k % nb
            p.dma("Y", f"lnx{b}", xf[b][:, :], x_dram[128 * k:128 * (k + 1), :],
                  writes=(f"xf{b}",))
            p.op("V", lambda v, b=b: v.tensor_copy(xc[b][:, :], xf[b][:, :]),
                 reads=(f"xf{b}",), writes=(f"xc{b}",))
            p.op("S", lambda s, b=b: s.activation(sq[b][:, :], xf[b][:, :], AF.Square),
                 reads=(f"xf{b}",), writes=(f"sq{b}",))
            for h in range(2):
                p.op("T", lambda t, b=b, h=h, k=k: t.matmul(
                    pb[h][0:1, :], c["ones_col"][:, 0:1],
                    xc[b][:, 512 * h:512 * (h + 1)], start=(k == 0), stop=(k == NKT - 1)),
                    reads=(f"xc{b}", "ones_col"), writes=(f"pb{h}",))
                p.op("T", lambda t, b=b, h=h, k=k: t.matmul(
                    pb[2 + h][0:1, :], c["ones_col"][:, 0:1],
                    sq[b][:, 512 * h:512 * (h + 1)], start=(k == 0), stop=(k == NKT - 1)),
                    reads=(f"sq{b}", "ones_col"), writes=(f"pb{2 + h}",))
        emit_stats_fin(p, c, [pb[0], pb[1], pb[2], pb[3]], ("pb0", "pb1", "pb2", "pb3"))
    mr, rsr = c["m_row"], c["rs_row"]
    for h in range(2):
        p.op("T", lambda t, h=h: t.matmul(pb[4 + h][:, :], c["ones_row_f"][0:1, :],
             mr[:, 512 * h:512 * (h + 1)], start=True, stop=True),
             reads=("m_row", "ones_row"), writes=(f"pb{4 + h}",))
        p.op("T", lambda t, h=h: t.matmul(pb[6 + h][:, :], c["ones_row_f"][0:1, :],
             rsr[:, 512 * h:512 * (h + 1)], start=True, stop=True),
             reads=("rs_row", "ones_row"), writes=(f"pb{6 + h}",))
    t1 = c["t1"]
    for k in range(NKT):
        b = k % nb
        p.dma("Y", f"lnx{b}", xf[b][:, :], x_dram[128 * k:128 * (k + 1), :],
              writes=(f"xf{b}",))
        for h in range(2):
            sl = slice(512 * h, 512 * (h + 1))
            osl = slice(1024 * k + 512 * h, 1024 * k + 512 * (h + 1))
            p.op("V", lambda v, b=b, sl=sl: v.tensor_sub(
                t1[b][:, sl], xf[b][:, sl], pb[4 + (sl.start // 512)][:, :]),
                reads=(f"xf{b}", f"pb{4 + h}"), writes=(f"t1{b}",))
            if want_f32:
                p.op("V", lambda v, b=b, sl=sl, osl=osl, h=h: v.tensor_mul(
                    c["xlnf"][b][:, sl], t1[b][:, sl], pb[6 + h][:, :]),
                    reads=(f"t1{b}", f"pb{6 + h}"), writes=(f"xlnf_s{b}",))
                p.op("V", lambda v, b=b, sl=sl, osl=osl: v.tensor_copy(c["xln"][:, osl], c["xlnf"][b][:, sl]),
                     reads=(f"xlnf_s{b}",), writes=(f"xln_{k}",))
            else:
                p.op("V", lambda v, b=b, sl=sl, osl=osl, h=h: v.tensor_mul(
                    c["xln"][:, osl], t1[b][:, sl], pb[6 + h][:, :]),
                    reads=(f"t1{b}", f"pb{6 + h}"), writes=(f"xln_{k}",))
        if want_f32 and f32_sink is not None:
            f32_sink(k, b, f"xlnf_s{b}")


def xln_reads(k):
    return (f"xln_{k}",)


def ffn_io(p, sfx="", with_x=True):
    d = dict(wimg=p.din("wimg" + sfx, [NKT, 128, NKT * 128], F32),
             bbi=p.din("bbi" + sfx, [128, NKT], F32),
             wr=p.din("wr" + sfx, [128, NKT * 8], F32),
             br=p.din("br" + sfx, [8, 1], F32),
             aimg=p.din("aimg" + sfx, [128, NKT * 128], F32),
             bstk=p.din("bstk" + sfx, [128, NKT * 128], F32),
             patw=p.din("patw" + sfx, [8, 128], F32),
             idf=p.din("idf" + sfx, [128, 128], F32))
    if with_x:
        d["x"] = p.din("x" + sfx, [DM, TOK], F32)
        d["xo"] = p.dout("xo" + sfx, [DM, TOK], F32)
    return d


def build_ffn(variant, debug=False):
    p = Prog(f"ffn_{variant}_dbg" if debug else f"ffn_{variant}")
    io = ffn_io(p)
    croot = alloc_ln_root(p)
    emit_ln_consts(p, croot, p.din("onescol", [128, 1], BF16), p.din("onesrow", [1, 128], F32))
    emit_ffn(p, croot, io, variant, debug)
    return p


def emit_ffn(p, croot, io, variant, debug=False):
    p.scope_open()
    x, wimg, bbi, wr, br = io["x"], io["wimg"], io["bbi"], io["wr"], io["br"]
    aimg, bstk, patw, idf, xo = io["aimg"], io["bstk"], io["patw"], io["idf"], io["xo"]
    if debug:
        d_hsb = p.dout("d_hsb", [128, TOK], F32)
        d_lgs = p.dout("d_lgs", [8, TOK], F32)
        d_wT = p.dout("d_wT", [8, TOK], F32)
        d_gsb = p.dout("d_gsb", [128, TOK], F32)
        d_y0 = p.dout("d_y0", [128, TOK], F32)
        d_xln0 = p.dout("d_xln0", [128, TOK], F32)
        d_ztok = p.dout("d_ztok", [128, 64], F32)
        d_wtok = p.dout("d_wtok", [128, 64], F32)

    stats_in, stats_out = io.get("stats_in"), io.get("stats_out")
    c = stage_ln(p, croot)
    pb = c["pb"]

    wt = [p.sb(f"wt{b}", [128, NKT * 128], BF16) for b in range(2)]
    at = p.sb("at", [128, NKT * 128], BF16)
    wrt = p.sb("wrt", [128, NKT * 8], BF16)
    bst = p.sb("bst", [128, NKT * 128], BF16)
    patt = p.sb("patt", [8, 128], F32)
    idt = p.sb("idt", [128, 128], F32)
    brt = p.sb("brt", [8, 1], F32)
    bbs = p.sb("bbs", [128, NKT], F32)
    hsb = p.sb("hsb", [128, TOK], F32)
    gsb = p.sb("gsb", [128, TOK], BF16)
    lgs = p.sb("lgs", [128, TOK], F32)
    ztok = p.sb("ztok", [128, 64], F32)
    wtokf = p.sb("wtokf", [128, 64], F32)
    wT = p.sb("wT", [8, TOK], F32)
    rr = [p.sb(f"rr{i}", [128, 8], F32) for i in range(4)]
    ysb = [p.sb(f"ysb{h}", [128, 512], F32) for h in range(2)]
    nxr = 4 if stats_out else 2
    xr = [p.sb(f"xr{b}", [128, TOK], F32) for b in range(nxr)]
    stn = ("pb4", "pb5", "pb6", "pb7")

    def emit_stats_m(ms):
        bs = ms % nxr
        for h in range(2):
            emit_stats_tile(p, croot, pb[4:8], stn, xr[bs][:, 512 * h:512 * (h + 1)],
                            f"xr{bs}", ms, h)

    emit_ln(p, c, x, stats_ready=stats_in)
    xln = c["xln"]

    p.dma("G", "swa", at[:, :], aimg, writes=("at",))
    p.dma("G", "swr", wrt[:, :], wr, writes=("wrt",))
    p.dma("G", "swb", bst[:, :], bstk, writes=("bst",))
    p.dma("Y", "cp", patt[:, :], patw, writes=("patt",))
    p.dma("Y", "cid", idt[:, :], idf, writes=("idt",))
    p.dma("Y", "cbr", brt[:, :], br, writes=("brt",))
    p.dma("Y", "cbb", bbs[:, :], bbi, writes=("bbs",))

    for h in range(2):
        for k in range(NKT):
            p.op("T", lambda t, h=h, k=k: t.matmul(
                pb[h][:, :], at[:, 128 * k:128 * (k + 1)],
                xln[:, 1024 * k + 512 * h:1024 * k + 512 * (h + 1)],
                start=(k == 0), stop=(k == NKT - 1)),
                reads=("at",) + xln_reads(k), writes=(f"pb{h}",))
        p.op("V", lambda v, h=h: v.tensor_copy(hsb[:, 512 * h:512 * (h + 1)], pb[h][:, :]),
             reads=(f"pb{h}",), writes=("hsb",))
    for h in range(2):
        for k in range(NKT):
            p.op("T", lambda t, h=h, k=k: t.matmul(
                pb[2 + h][0:8, :], wrt[:, 8 * k:8 * (k + 1)],
                xln[:, 1024 * k + 512 * h:1024 * k + 512 * (h + 1)],
                start=(k == 0), stop=(k == NKT - 1)),
                reads=("wrt",) + xln_reads(k), writes=(f"pb{2 + h}",))
        p.op("V", lambda v, h=h: v.tensor_scalar(lgs[0:8, 512 * h:512 * (h + 1)],
             pb[2 + h][0:8, :], brt[:, 0:1], None, ALU.add),
             reads=(f"pb{2 + h}", "brt"), writes=("lgs",))

    for j in range(8):
        p.op("T", lambda t, j=j: t.transpose(pb[4][:, 0:128], lgs[:, 128 * j:128 * (j + 1)], idt[:, :]),
             reads=("lgs", "idt"), writes=("pb4",))
        p.op("V", lambda v, j=j: v.tensor_copy(ztok[:, 8 * j:8 * (j + 1)], pb[4][:, 0:8]),
             reads=("pb4",), writes=(f"ztok{j}",))
        z = ztok[:, 8 * j:8 * (j + 1)]
        wv = wtokf[:, 8 * j:8 * (j + 1)]
        if variant == "l0":
            p.op("V", lambda v, z=z: v.reduce_max(rr[0][:, 0:1], z, AX.X, negate=True),
                 reads=(f"ztok{j}",), writes=("rr0",))
            p.op("V", lambda v, z=z: v.tensor_scalar(rr[1][:, 0:8], z, rr[0][:, 0:1], None, ALU.add),
                 reads=(f"ztok{j}",), sreads=("rr0",), writes=("rr1",))
            p.op("V", lambda v: v.tensor_scalar(rr[2][:, 0:8], rr[1][:, 0:8], 0.0, -1e30, ALU.is_ge, ALU.mult),
                 reads=("rr1",), writes=("rr2",))
            p.op("V", lambda v: v.tensor_add(rr[2][:, 0:8], rr[1][:, 0:8], rr[2][:, 0:8]),
                 reads=("rr1", "rr2"), writes=("rr2",))
            p.op("V", lambda v: v.reduce_max(rr[3][:, 0:1], rr[2][:, 0:8], AX.X),
                 reads=("rr2",), writes=("rr3",))
            p.op("V", lambda v: v.tensor_scalar(rr[3][:, 0:8], rr[1][:, 0:8], rr[3][:, 0:1], None, ALU.is_ge),
                 reads=("rr1",), sreads=("rr3",), writes=("rr3",))
            p.op("S", lambda s: s.activation(rr[1][:, 0:8], rr[1][:, 0:8], AF.Exp),
                 reads=("rr1",), writes=("rr1",))
            p.op("V", lambda v: v.tensor_mul(rr[1][:, 0:8], rr[1][:, 0:8], rr[3][:, 0:8]),
                 reads=("rr1", "rr3"), writes=("rr1",))
            p.op("V", lambda v: v.reduce_sum(rr[2][:, 0:1], rr[1][:, 0:8], AX.X),
                 reads=("rr1",), writes=("rr2",))
            p.op("V", lambda v: v.reciprocal(rr[2][:, 0:1], rr[2][:, 0:1]),
                 reads=("rr2",), writes=("rr2",))
            p.op("V", lambda v, wv=wv: v.tensor_scalar(wv, rr[1][:, 0:8], rr[2][:, 0:1], 2.0, ALU.mult, ALU.mult),
                 reads=("rr1",), sreads=("rr2",), writes=("wtokf",))
        else:
            p.op("V", lambda v: v.memset(rr[0][:, 0:8], 0.0), writes=("rr0",))
            p.op("V", lambda v: v.memset(rr[1][:, 0:8], 0.0), writes=("rr1",))
            for jj in range(8):
                zj = ztok[:, 8 * j + jj:8 * j + jj + 1]
                p.op("V", lambda v, z=z, zj=zj: v.scalar_tensor_tensor(
                    rr[0][:, 0:8], z, zj, rr[0][:, 0:8], ALU.is_lt, ALU.add),
                    reads=("rr0",), sreads=(f"ztok{j}",), writes=("rr0",))
                p.op("V", lambda v, z=z, zj=zj: v.tensor_scalar(
                    rr[2][:, 0:8], z, zj, zj, ALU.is_le, ALU.mult),
                    sreads=(f"ztok{j}",), writes=("rr2",))
                p.op("V", lambda v: v.tensor_add(rr[1][:, 0:8], rr[1][:, 0:8], rr[2][:, 0:8]),
                     reads=("rr1", "rr2"), writes=("rr1",))
            p.op("V", lambda v: v.tensor_scalar(rr[0][:, 0:8], rr[0][:, 0:8], 1.0, None, ALU.add),
                 reads=("rr0",), writes=("rr0",))
            p.op("V", lambda v, z=z: v.tensor_mul(rr[2][:, 0:8], rr[0][:, 0:8], z),
                 reads=("rr0", f"ztok{j}"), writes=("rr2",))
            p.op("V", lambda v: v.scalar_tensor_tensor(rr[2][:, 0:8], rr[2][:, 0:8], 1.0, rr[1][:, 0:8], ALU.add, ALU.subtract),
                 reads=("rr2", "rr1"), writes=("rr2",))
            p.op("V", lambda v: v.tensor_scalar(rr[2][:, 0:8], rr[2][:, 0:8], 0.0, None, ALU.is_gt),
                 reads=("rr2",), writes=("rr2",))
            p.op("V", lambda v, z=z: v.tensor_mul(rr[3][:, 0:8], rr[2][:, 0:8], z),
                 reads=("rr2", f"ztok{j}"), writes=("rr3",))
            p.op("V", lambda v: v.reduce_sum(rr[3][:, 0:1], rr[3][:, 0:8], AX.X),
                 reads=("rr3",), writes=("rr3",))
            p.op("V", lambda v: v.reduce_sum(rr[2][:, 0:1], rr[2][:, 0:8], AX.X),
                 reads=("rr2",), writes=("rr2",))
            p.op("V", lambda v: v.reciprocal(rr[2][:, 0:1], rr[2][:, 0:1]),
                 reads=("rr2",), writes=("rr2",))
            p.op("V", lambda v: v.tensor_scalar(rr[3][:, 0:1], rr[3][:, 0:1], 1.0, None, ALU.subtract),
                 reads=("rr3",), writes=("rr3",))
            p.op("V", lambda v: v.tensor_mul(rr[3][:, 0:1], rr[3][:, 0:1], rr[2][:, 0:1]),
                 reads=("rr3", "rr2"), writes=("rr3",))
            p.op("V", lambda v, z=z, wv=wv: v.tensor_scalar(wv, z, rr[3][:, 0:1], 0.0, ALU.subtract, ALU.max),
                 reads=(f"ztok{j}",), sreads=("rr3",), writes=("wtokf",))
    for j in range(8):
        p.op("T", lambda t, j=j: t.transpose(pb[5][0:8, 0:128], wtokf[:, 8 * j:8 * (j + 1)], idt[:, :]),
             reads=("wtokf", "idt"), writes=("pb5",))
        p.op("V", lambda v, j=j: v.tensor_copy(wT[:, 128 * j:128 * (j + 1)], pb[5][0:8, 0:128]),
             reads=("pb5",), writes=("wT",))

    for h in range(2):
        p.op("T", lambda t, h=h: t.matmul(pb[6 + h][:, :], patt[:, :],
             wT[:, 512 * h:512 * (h + 1)], start=True, stop=True),
             reads=("patt", "wT"), writes=(f"pb{6 + h}",))
        p.op("V", lambda v, h=h: v.tensor_mul(gsb[:, 512 * h:512 * (h + 1)],
             hsb[:, 512 * h:512 * (h + 1)], pb[6 + h][:, :]),
             reads=("hsb", f"pb{6 + h}"), writes=("gsb",))

    if debug:
        p.dma("Y", "dbg8", d_ztok, ztok[:, :], reads=tuple(f"ztok{j}" for j in range(8)), out_sem=True)
        p.dma("Y", "dbg9", d_wtok, wtokf[:, :], reads=("wtokf",), out_sem=True)
        dbf = p.sb("dbf", [128, TOK], F32)
        p.dma("Y", "dbg1", d_hsb, hsb[:, :], reads=("hsb",), out_sem=True)
        p.dma("Y", "dbg2", d_lgs, lgs[0:8, :], reads=("lgs",), out_sem=True)
        p.dma("Y", "dbg3", d_wT, wT[:, :], reads=("wT",), out_sem=True)
        p.op("V", lambda v: v.tensor_copy(dbf[:, :], gsb[:, :]), reads=("gsb",), writes=("dbf",))
        p.dma("Y", "dbg4", d_gsb, dbf[:, :], reads=("dbf",), out_sem=True)
        p.op("V", lambda v: v.tensor_copy(dbf[:, :], xln[:, 0:TOK]), reads=("xln_0",), writes=("dbf2",))
        p.dma("Y", "dbg5", d_xln0, dbf[:, :], reads=("dbf2",), out_sem=True)

    for m in range(NKT):
        b = m % 2
        p.dma("G", f"w{b}", wt[b][:, :], wimg[m], writes=(f"wt{b}",))
        for h in range(2):
            bank = (m % 2) * 2 + h
            for k in range(NKT):
                p.op("T", lambda t, b=b, h=h, k=k, bank=bank: t.matmul(
                    pb[bank][:, :], wt[b][:, 128 * k:128 * (k + 1)],
                    xln[:, 1024 * k + 512 * h:1024 * k + 512 * (h + 1)],
                    start=(k == 0), stop=False),
                    reads=(f"wt{b}",) + xln_reads(k), writes=(f"pb{bank}",))
            p.op("T", lambda t, h=h, m=m, bank=bank: t.matmul(
                pb[bank][:, :], bst[:, 128 * m:128 * (m + 1)],
                gsb[:, 512 * h:512 * (h + 1)], start=False, stop=True),
                reads=("bst", "gsb"), writes=(f"pb{bank}",))
            p.op("V", lambda v, h=h, m=m, bank=bank: v.tensor_scalar(
                ysb[h][:, :], pb[bank][:, :], bbs[:, m:m + 1], None, ALU.add),
                reads=(f"pb{bank}", "bbs"), writes=(f"ysb{h}",))
        bx = m % nxr
        p.dma("Y", f"xr{bx}", xr[bx][:, :], x[128 * m:128 * (m + 1), :], writes=(f"xr{bx}",))
        for h in range(2):
            p.op("V", lambda v, bx=bx, h=h: v.tensor_add(
                xr[bx][:, 512 * h:512 * (h + 1)], xr[bx][:, 512 * h:512 * (h + 1)], ysb[h][:, :]),
                reads=(f"xr{bx}", f"ysb{h}"), writes=(f"xr{bx}",))
        p.dma("Y", f"xo{bx}", xo[128 * m:128 * (m + 1), :], xr[bx][:, :],
              reads=(f"xr{bx}",), out_sem=True)
        if stats_out and m >= 2:
            emit_stats_m(m - 2)
        if debug and m == 0:
            p.dma("Y", "dbg6", d_y0[:, 0:512], ysb[0][:, :], reads=("ysb0",), out_sem=True)
            p.dma("Y", "dbg7", d_y0[:, 512:1024], ysb[1][:, :], reads=("ysb1",), out_sem=True)
    if stats_out:
        emit_stats_m(NKT - 2)
        emit_stats_m(NKT - 1)
        emit_stats_fin(p, croot, pb[4:8], stn)
    p.scope_close()


def host_ffn_consts(variant, W_base, b_base, W_router, b_router, lora_A, lora_B):
    MATW = np.ascontiguousarray(W_base.T)
    wimg = blocked_img(MATW)
    wr = small_img(np.ascontiguousarray(W_router.T))
    if variant == "l0":
        A = lora_A.transpose(2, 0, 1).reshape(DM, 128)
        Bm = lora_B.transpose(0, 2, 1).reshape(128, DM)
    else:
        A = lora_A.transpose(1, 0, 2).reshape(DM, 128)
        Bm = lora_B.reshape(128, DM)
    aimg = blocked_img(np.ascontiguousarray(A))[0]
    bstk = np.ascontiguousarray(Bm)
    patw = np.zeros((8, 128), dtype=np.float32)
    for e in range(8):
        patw[e, 16 * e:16 * (e + 1)] = 1
    return dict(
        wimg=wimg.astype(np.float32),
        bbi=np.ascontiguousarray(b_base.reshape(NKT, 128).T).astype(np.float32),
        wr=wr.astype(np.float32),
        br=(b_router if b_router is not None else np.zeros(8)).reshape(8, 1).astype(np.float32),
        aimg=aimg.astype(np.float32), bstk=bstk.astype(np.float32), patw=patw,
        idf=np.eye(128, dtype=np.float32),
        onescol=np.ones((128, 1), dtype=BF),
        onesrow=np.ones((1, 128), dtype=np.float32),
    )




def build_attn_pre(wdt="f32"):
    p = Prog(f"attn_pre_{wdt}")
    io = dict(x=p.din("x", [DM, TOK], F32),
              wimg=p.din("wimg", [48, 128, NKT * 128], F32 if wdt == "f32" else BF16),
              qkvp=p.dout("qkvp", [6144, TOK], BF16))
    croot = alloc_ln_root(p)
    emit_ln_consts(p, croot, p.din("onescol", [128, 1], BF16), p.din("onesrow", [1, 128], F32))
    emit_attn_pre(p, croot, io)
    return p


def emit_attn_pre(p, croot, io):
    p.scope_open()
    x, wimg, qkvp = io["x"], io["wimg"], io["qkvp"]
    c = stage_ln(p, croot)
    wt = [p.sb(f"wt{b}", [128, NKT * 128], BF16) for b in range(2)]
    ob = [p.sb(f"ob{b}", [128, TOK], BF16) for b in range(2)]
    emit_ln(p, c, x, stats_ready=io.get("stats_in"))
    xln, pb = c["xln"], c["pb"]
    for m in range(48):
        b = m % 2
        p.dma("G", f"w{b}", wt[b][:, :], wimg[m], writes=(f"wt{b}",))
        for h in range(2):
            bank = (m % 2) * 2 + h
            for k in range(NKT):
                p.op("T", lambda t, b=b, h=h, k=k, bank=bank: t.matmul(
                    pb[bank][:, :], wt[b][:, 128 * k:128 * (k + 1)],
                    xln[:, 1024 * k + 512 * h:1024 * k + 512 * (h + 1)],
                    start=(k == 0), stop=(k == NKT - 1)),
                    reads=(f"wt{b}",) + xln_reads(k), writes=(f"pb{bank}",))
            p.op("S", lambda s, b=b, h=h, bank=bank: s.copy(
                ob[b][:, 512 * h:512 * (h + 1)], pb[bank][:, :]),
                reads=(f"pb{bank}",), writes=(f"ob{b}",))
        p.dma("Y", f"o{b}", qkvp[128 * m:128 * (m + 1), :], ob[b][:, :],
              reads=(f"ob{b}",), out_sem=True)
    p.scope_close()


NQT = 16


def build_attn_core():
    p = Prog("attn_core")
    io = dict(qkvm=p.din("qkvm", [6144, TOK], BF16),
              cosf=p.din("cosf", [128, SEQ], F32),
              sinf=p.din("sinf", [128, SEQ], F32),
              trimask=p.din("trimask", [4, 128, 512], BF16),
              onescol=p.din("onescol", [128, 1], BF16),
              onesrow=p.din("onesrow", [1, 128], F32),
              aout=p.dout("aout", [DM, TOK], BF16))
    emit_attn_core(p, io)
    return p


def emit_attn_core(p, io):
    croot = io["croot"]
    """Per-core attention: 4 q-heads + 1 kv-head over full sequence.
    qkvm: [8*768, 1024] bf16 (rank-major blocks).
    aout [4096, 1024] bf16: row 512*j + 128*h + hd for token block j."""
    p.scope_open()
    qkvm, cosf, sinf, trimask = io["qkvm"], io["cosf"], io["sinf"], io["trimask"]
    ones_col_d, ones_row_d, aout = io["onescol"], io["onesrow"], io["aout"]
    dend = io["dend"]

    qh = [p.sb(f"qh{b}", [128, SEQ], BF16) for b in range(2)]
    kh = p.sb("kh", [128, SEQ], BF16)
    qr = [p.sb(f"qr{b}", [128, SEQ], BF16) for b in range(2)]
    kr = p.sb("kr", [128, SEQ], BF16)
    vtok = p.sb("vtok", [128, SEQ], BF16)
    cost = p.sb("cost", [128, SEQ], BF16)
    sint = p.sb("sint", [128, SEQ], BF16)
    tmask = p.sb("tmask", [128, 4 * 512], BF16)
    ones_col = p.sb("ones_col2", [128, 1], BF16)
    ones_row_f = p.sb("ones_row2", [1, 128], F32)
    t2 = p.sb("t2", [128, SEQ // 4], BF16)
    den = [croot["tmp_row"][0:1, 512 * i:512 * (i + 1)] for i in range(2)]
    ncols = p.sb("ncols", [1, 16], F32)
    sqn = [p.sb(f"sqn{i}", [128, 512], BF16) for i in range(2)]
    mrow = p.sb("mrow", [1, 8], F32)
    negm = p.sb("negm", [128, 2], F32)
    psb = [p.sb(f"psb{i}", [128, 1024], BF16) for i in range(2)]
    osb = croot["st_xb"]
    ps_s = [p.psum(f"ps_s{i}", [128, 1024], F32) for i in range(2)]
    ps_av = [p.psum(f"ps_av{i}", [128, 512], F32) for i in range(2)]
    ps_d = [p.psum(f"ps_d{i}", [1, 512], F32) for i in range(2)]
    ps_n = ps_d[1]

    p.dma("G", "ccos", cost[:, :], cosf, writes=("cost",))
    p.dma("G", "csin", sint[:, :], sinf, writes=("sint",))
    p.dma("Y", "cmask", tmask[:, :].rearrange("p (j f) -> p j f", j=4), trimask.rearrange("j p f -> p j f"), writes=("tmask",))
    p.dma("Y", "cones", ones_col[:, :], ones_col_d, writes=("ones_col2",))
    p.dma("Y", "cones2", ones_row_f[:, :], ones_row_d, writes=("ones_row2",))

    def load_headrows(dst, dname, base, sem):
        for i in range(NCORE):
            p.dma("Y", sem, dst[:, 1024 * i:1024 * (i + 1)],
                  qkvm[768 * i + base:768 * i + base + 128, :], writes=(f"{dname}_{i}",))

    def rd(dname):
        return tuple(f"{dname}_{i}" for i in range(NCORE))

    def rope(dst_name, dst, src_name, src):
        srcr = rd(src_name)
        H = SEQ // 4
        for g in range(4):
            sl = slice(H * g, H * (g + 1))
            p.op("V", lambda v, sl=sl: v.tensor_mul(t2[0:64, :], src[64:128, sl], sint[64:128, sl]),
                 reads=srcr + ("sint",), writes=("t2a",))
            p.op("V", lambda v, sl=sl: v.tensor_mul(t2[64:128, :], src[0:64, sl], sint[0:64, sl]),
                 reads=srcr + ("sint",), writes=("t2b",))
            p.op("V", lambda v, sl=sl: v.tensor_mul(dst[:, sl], src[:, sl], cost[:, sl]),
                 reads=srcr + ("cost",), writes=(dst_name + "_m",))
            p.op("V", lambda v, sl=sl: v.tensor_add(dst[:, sl], dst[:, sl], t2[:, :]),
                 reads=(dst_name + "_m", "t2a", "t2b"), writes=(dst_name,))

    def norms(src_name, src, out_col):
        for i in range(NQT):
            b = i % 2
            p.op("S", lambda s, i=i, b=b: s.activation(sqn[b][:, :], src[:, 512 * i:512 * (i + 1)], AF.Square),
                 reads=(src_name,), writes=(f"sqn{b}",))
            p.op("T", lambda t, b=b: t.matmul(ps_d[b][0:1, :], ones_col[:, 0:1],
                 sqn[b][:, :], start=True, stop=True),
                 reads=(f"sqn{b}", "ones_col2"), writes=(f"ps_d{b}",))
            p.op("V", lambda v, i=i, b=b: v.reduce_max(ncols[0:1, i:i + 1], ps_d[b][0:1, :], AX.X),
                 reads=(f"ps_d{b}",), writes=("ncols",))
        p.op("V", lambda v, oc=out_col: v.reduce_max(mrow[0:1, oc:oc + 1], ncols[0:1, :], AX.X),
             reads=("ncols",), writes=(f"mcol{out_col}",))

    load_headrows(kh, "kh", 512, "lk")
    norms("kh", kh, 0)
    rope("kr", kr, "kh", kh)
    for i in range(64):
        p.dma("S", "lv", vtok[:, 128 * i:128 * (i + 1)],
              qkvm[768 * (i // 8) + 640:768 * (i // 8) + 768, 128 * (i % 8):128 * (i % 8 + 1)],
              transpose=True, writes=(f"vtok_{i // 8}",))

    def prep_head(hq):
        b = hq % 2
        load_headrows(qh[b], f"qh{b}", 128 * hq, f"lq{b}")
        norms(f"qh{b}", qh[b], 1)
        rope(f"qr{b}", qr[b], f"qh{b}", qh[b])
        p.op("V", lambda v: v.tensor_mul(mrow[0:1, 2:3], mrow[0:1, 0:1], mrow[0:1, 1:2]),
             reads=("mcol0", "mcol1"), writes=("m2c",))
        p.op("S", lambda s: s.activation(mrow[0:1, 3:4], mrow[0:1, 2:3], AF.Sqrt),
             reads=("m2c",), writes=("m3c",))
        p.op("V", lambda v: v.tensor_scalar(mrow[0:1, 4:5], mrow[0:1, 3:4], -0.08838834764831845, None, ALU.mult),
             reads=("m3c",), writes=("m4c",))
        p.op("T", lambda t: t.matmul(ps_av[1][:, 0:1], ones_row_f[0:1, :], mrow[0:1, 4:5], start=True, stop=True),
             reads=("m4c", "ones_row2"), writes=("ps_av1",))
        p.op("V", lambda v, b=b: v.tensor_copy(negm[:, b:b + 1], ps_av[1][:, 0:1]),
             reads=("ps_av1",), writes=(f"negm{b}",))

    prep_head(0)
    for hq in range(4):
        hb = hq % 2
        for qt in range(NQT):
            if qt == 8 and hq < 3:
                prep_head(hq + 1)
            ntk = 4 * qt + 4
            npair = ntk // 2

            def emit_sc(pr, qt=qt, hb=hb):
                sb = pr % 2
                for u in range(2):
                    i = 2 * pr + u
                    p.op("T", lambda t, i=i, qt=qt, sb=sb, u=u, hb=hb: t.matmul(
                        ps_s[sb][:, 512 * u:512 * (u + 1)], kr[:, 128 * i:128 * (i + 1)],
                        qr[hb][:, 512 * qt:512 * (qt + 1)], start=True, stop=True),
                        reads=("kr", f"qr{hb}"), writes=(f"ps_s{sb}_{u}",))
                p.op("S", lambda s, sb=sb, hb=hb: s.activation(psb[sb][:, :], ps_s[sb][:, :], AF.Exp,
                     bias=negm[:, hb:hb + 1], scale=0.08838834764831845),
                    reads=(f"ps_s{sb}_0", f"ps_s{sb}_1"), sreads=(f"negm{hb}",), writes=(f"psb{sb}",))
                for u in range(2):
                    i = 2 * pr + u
                    diag = i - 4 * qt
                    if diag >= 0:
                        p.op("V", lambda v, sb=sb, u=u, d=diag: v.tensor_mul(
                             psb[sb][:, 512 * u:512 * (u + 1)], psb[sb][:, 512 * u:512 * (u + 1)],
                             tmask[:, 512 * d:512 * (d + 1)]),
                             reads=(f"psb{sb}", "tmask"), writes=(f"psb{sb}",))

            def emit_dnav(pr, qt=qt, ntk=ntk):
                sb = pr % 2
                for u in range(2):
                    i = 2 * pr + u
                    p.op("T", lambda t, sb=sb, u=u, i=i, ntk=ntk, qt=qt: t.matmul(
                         ps_d[qt % 2][0:1, :], ones_col[:, 0:1],
                         psb[sb][:, 512 * u:512 * (u + 1)], start=(i == 0), stop=(i == ntk - 1)),
                         reads=(f"psb{sb}", "ones_col2"), writes=(f"ps_d{qt % 2}",))
                    p.op("T", lambda t, sb=sb, u=u, i=i, ntk=ntk, qt=qt: t.matmul(
                         ps_av[qt % 2][:, :],
                         vtok[:, 128 * i:128 * (i + 1)], psb[sb][:, 512 * u:512 * (u + 1)],
                         start=(i == 0), stop=(i == ntk - 1)),
                         reads=(f"psb{sb}", f"vtok_{i // 8}"), writes=(f"ps_av{qt % 2}",))

            for pr in range(npair):
                emit_sc(pr)
                if pr >= 1:
                    emit_dnav(pr - 1)
            emit_dnav(npair - 1)
            b2 = qt % 2
            p.op("V", lambda v, b2=b2: v.tensor_copy(osb[b2][:, :], ps_av[b2][:, :]),
                 reads=(f"ps_av{b2}",), writes=(f"osb{b2}",))
            p.op("V", lambda v, b2=b2: v.tensor_copy(den[b2][0:1, :], ps_d[b2][0:1, :]),
                 reads=(f"ps_d{b2}",), writes=(f"den{b2}",))
            j, half = qt // 2, qt % 2
            p.dma("Y", f"oa{qt % 2}", aout[512 * j + 128 * hq:512 * j + 128 * (hq + 1),
                  512 * half:512 * (half + 1)], osb[b2][:, :], reads=(f"osb{b2}",), out_sem=True)
            p.dma("Y", f"od{qt % 2}", dend[8 * j + 2 * hq + half:8 * j + 2 * hq + half + 1, :],
                  den[b2][0:1, :], reads=(f"den{b2}",), out_sem=True)
    p.scope_close()


def build_attn_post(wdt="f32"):
    p = Prog(f"attn_post_{wdt}")
    io = dict(x=p.din("x", [DM, TOK], F32),
              at=p.din("at", [DM, TOK], BF16),
              wimg=p.din("wimg", [NKT, 128, NKT * 128], F32 if wdt == "f32" else BF16),
              xo=p.dout("xo", [DM, TOK], F32))
    emit_attn_post(p, io)
    return p


def emit_post_rcp_prologue(p, io, att, pb):
    dA, dAr, onesrow = io["dA"], io["dAr"], io["onesrow"]
    denb = p.sb("denb", [64, 512], F32)
    denr = p.sb("denr", [64, 512], F32)
    rc1 = [p.sb(f"rc1_{b}", [1, 1024], F32) for b in range(2)]
    onr = p.sb("onr2", [1, 128], F32)
    p.dma("Y", "dld", denb[:, :], dA, writes=("denb",))
    p.dma("Y", "dld", onr[:, :], onesrow, writes=("onr2",))
    p.op("V", lambda v: v.reciprocal(denr[:, :], denb[:, :]),
         reads=("denb",), writes=("denr",))
    p.dma("Y", "dst", dAr.rearrange("b (g f) -> (b g) f", g=2), denr[:, :],
          reads=("denr",), writes=("dAr_d",))

    def scale_tile(k):
        b = k % 2
        p.dma("Y", f"rl{b}", rc1[b][:, :], dAr[k:k + 1, :],
              reads=("dAr_d",), writes=(f"rc1{b}",))
        for h in range(2):
            p.op("T", lambda t, b=b, h=h: t.matmul(pb[h][:, :], onr[0:1, :],
                 rc1[b][0:1, 512 * h:512 * (h + 1)], start=True, stop=True),
                 reads=(f"rc1{b}", "onr2"), writes=(f"pb{h}",))
            p.op("V", lambda v, k=k, h=h: v.tensor_mul(
                 att[:, 1024 * k + 512 * h:1024 * k + 512 * (h + 1)],
                 att[:, 1024 * k + 512 * h:1024 * k + 512 * (h + 1)], pb[h][:, :]),
                 reads=(f"att{k}", f"pb{h}"), writes=(f"att{k}",))
    return scale_tile


def emit_attn_post(p, io):
    p.scope_open()
    x, at, wimg, xo = io["x"], io["at"], io["wimg"], io["xo"]
    stats_out = io.get("stats_out")
    croot = io.get("croot")
    wt = [p.sb(f"wt{b}", [128, NKT * 128], BF16) for b in range(2)]
    att = p.sb("att", [128, NKT * TOK], BF16)
    nxr = 4 if stats_out else 2
    xr = [p.sb(f"xr{b}", [128, TOK], F32) for b in range(nxr)]
    pb = [p.psum(f"pb{i}", [128, 512], F32) for i in range(4)]
    if stats_out:
        st = [p.psum(f"stp{i}", [1, 512], F32) for i in range(4)]
        stn = ("st0", "st1", "st2", "st3")

    def emit_stats_m(ms):
        bs = ms % nxr
        for h in range(2):
            emit_stats_tile(p, croot, st, stn, xr[bs][:, 512 * h:512 * (h + 1)],
                            f"xr{bs}", ms, h)
    scale_tile = emit_post_rcp_prologue(p, io, att, pb)
    for k in range(NKT):
        p.dma("Y", f"a{k % 2}", att[:, 1024 * k:1024 * (k + 1)], at[128 * k:128 * (k + 1), :],
              writes=(f"att{k}",))
        scale_tile(k)
    for m in range(NKT):
        b = m % 2
        bx = m % nxr
        p.dma("G", f"w{b}", wt[b][:, :], wimg[m], writes=(f"wt{b}",))
        for h in range(2):
            bank = (m % 2) * 2 + h
            for k in range(NKT):
                p.op("T", lambda t, b=b, h=h, k=k, bank=bank: t.matmul(
                    pb[bank][:, :], wt[b][:, 128 * k:128 * (k + 1)],
                    att[:, 1024 * k + 512 * h:1024 * k + 512 * (h + 1)],
                    start=(k == 0), stop=(k == NKT - 1)),
                    reads=(f"wt{b}", f"att{k}"), writes=(f"pb{bank}",))
        p.dma("Y", f"xr{bx}", xr[bx][:, :], x[128 * m:128 * (m + 1), :], writes=(f"xr{bx}",))
        for h in range(2):
            bank = (m % 2) * 2 + h
            p.op("V", lambda v, bx=bx, h=h, bank=bank: v.tensor_add(
                xr[bx][:, 512 * h:512 * (h + 1)], xr[bx][:, 512 * h:512 * (h + 1)], pb[bank][:, :]),
                reads=(f"xr{bx}", f"pb{bank}"), writes=(f"xr{bx}",))
        p.dma("Y", f"xo{bx}", xo[128 * m:128 * (m + 1), :], xr[bx][:, :],
              reads=(f"xr{bx}",), out_sem=True)
        if stats_out and m >= 2:
            emit_stats_m(m - 2)
    if stats_out:
        emit_stats_m(NKT - 2)
        emit_stats_m(NKT - 1)
        emit_stats_fin(p, croot, st, stn)
    p.scope_close()


_PROG_CACHE = {}


def get_prog(key, builder):
    if key not in _PROG_CACHE:
        _PROG_CACHE[key] = builder().finish()
    return _PROG_CACHE[key]


def run_ffn(variant, xT_shards, consts):
    nc = get_prog(f"ffn_{variant}", lambda: build_ffn(variant))
    in_maps = [dict(consts, x=np.ascontiguousarray(s)) for s in xT_shards]
    res = run_bass_kernel_spmd(nc, in_maps, core_ids=list(range(NCORE)))
    return [r["xo"] for r in res.results]




def host_qkv_mat(Wq_mat, Wk_mat, Wv_mat):
    cols = []
    for c in range(NCORE):
        cols.append(Wq_mat[:, 512 * c:512 * (c + 1)])
        cols.append(Wk_mat[:, 128 * c:128 * (c + 1)])
        cols.append(Wv_mat[:, 128 * c:128 * (c + 1)])
    return np.concatenate(cols, axis=1)


def host_rope_tables(kind):
    if kind == "l0":
        HEAD_DIM = 128; ROPE_THETA = 500000.0; ROPE_FACTOR = 8.0
        HIGH_FF = 4.0; LOW_FF = 1.0; ORIG_MAX_POS = 8192
        inv = 1.0 / ROPE_THETA ** (np.arange(0, HEAD_DIM, 2, dtype=np.float32) / HEAD_DIM)
        wavelen = 2.0 * np.pi / inv
        low_wl = ORIG_MAX_POS / LOW_FF
        high_wl = ORIG_MAX_POS / HIGH_FF
        inv_scaled = np.where(wavelen > low_wl, inv / ROPE_FACTOR, inv)
        smooth = (ORIG_MAX_POS / wavelen - LOW_FF) / (HIGH_FF - LOW_FF)
        smoothed = (1.0 - smooth) / ROPE_FACTOR * inv + smooth * inv
        is_medium = (wavelen <= low_wl) & (wavelen >= high_wl)
        inv_freq = np.where(is_medium, smoothed, inv_scaled).astype(np.float32)
    else:
        inv_freq = (1.0 / 10000.0 ** (np.arange(0, 128, 2, dtype=np.float32) / 128)).astype(np.float32)
    pos = np.arange(SEQ, dtype=np.float32)
    freqs = pos[:, None] * inv_freq[None, :]
    emb = np.concatenate([freqs, freqs], axis=-1)
    cos = np.cos(emb).astype(np.float32).T
    sin = np.sin(emb).astype(np.float32).T
    sinf = np.concatenate([sin[64:128], -sin[0:64]], axis=0)
    return np.ascontiguousarray(cos), np.ascontiguousarray(sinf)


def host_trimask():
    m = np.zeros((4, 128, 512), dtype=BF)
    for d in range(4):
        for pp in range(128):
            m[d, pp, 128 * d + pp:] = 1.0
    return m


def host_attn_consts(kind, Wq, Wk, Wv, Wo):
    if kind == "l0":
        mq, mk, mv, mo = Wq.T, Wk.T, Wv.T, Wo.T
    else:
        mq, mk, mv, mo = Wq, Wk, Wv, Wo
    qkv = host_qkv_mat(np.ascontiguousarray(mq), np.ascontiguousarray(mk), np.ascontiguousarray(mv))
    cos, sinf = host_rope_tables("l0" if kind == "l0" else "l1")
    dt = np.float32 if qkv.dtype == np.float32 else BF
    pre = dict(wimg=blocked_img(qkv).astype(dt),
               onescol=np.ones((128, 1), dtype=BF),
               onesrow=np.ones((1, 128), dtype=np.float32))
    core = dict(cosf=cos, sinf=sinf, trimask=host_trimask(),
                onescol=np.ones((128, 1), dtype=BF),
                onesrow=np.ones((1, 128), dtype=np.float32))
    post = dict(wimg=blocked_img(np.ascontiguousarray(mo)).astype(dt))
    return pre, core, post


def run_attn(kind, xT_shards, consts3, wdt="f32"):
    pre_c, core_c, post_c = consts3
    nc1 = get_prog(f"attn_pre_{wdt}", lambda: build_attn_pre(wdt))
    res1 = run_bass_kernel_spmd(nc1, [dict(pre_c, x=np.ascontiguousarray(s)) for s in xT_shards],
                                core_ids=list(range(NCORE)))
    qkvp = [r["qkvp"] for r in res1.results]
    nc2 = get_prog("attn_core", lambda: build_attn_core())
    in2 = []
    for c in range(NCORE):
        mine = np.concatenate([np.asarray(qkvp[i])[768 * c:768 * (c + 1)] for i in range(NCORE)], axis=0)
        in2.append(dict(core_c, qkvm=np.ascontiguousarray(mine)))
    res2 = run_bass_kernel_spmd(nc2, in2, core_ids=list(range(NCORE)))
    aout = [r["aout"] for r in res2.results]
    nc3 = get_prog(f"attn_post_{wdt}", lambda: build_attn_post(wdt))
    in3 = []
    for c in range(NCORE):
        at = np.concatenate([np.asarray(aout[i])[512 * c:512 * (c + 1)] for i in range(NCORE)], axis=0)
        in3.append(dict(post_c, x=np.ascontiguousarray(xT_shards[c]), at=np.ascontiguousarray(at)))
    res3 = run_bass_kernel_spmd(nc3, in3, core_ids=list(range(NCORE)))
    return [r["xo"] for r in res3.results]




def build_ssm_pre():
    p = Prog("ssm_pre")
    io = dict(x=p.din("x", [DM, TOK], F32),
              wimg=p.din("wimg", [2, 128, NKT * 128], F32),
              xlno=p.dout("xlno", [DM, TOK], F32),
              xpo=p.dout("xpo", [160, TOK], F32))
    croot = alloc_ln_root(p)
    emit_ln_consts(p, croot, p.din("onescol", [128, 1], BF16), p.din("onesrow", [1, 128], F32))
    emit_ssm_pre(p, croot, io)
    return p


def emit_ssm_pre(p, croot, io):
    p.scope_open()
    x, wimg, xlno, xpo = io["x"], io["wimg"], io["xlno"], io["xpo"]
    c = stage_ln(p, croot, want_f32=True)
    wt = [p.sb(f"wt{b}", [128, NKT * 128], BF16) for b in range(2)]
    xp1 = p.sb("xp1", [128, TOK], F32)
    xp2 = p.sb("xp2", [32, TOK], F32)
    def sink(k, b, res):
        p.dma("Y", f"xl{b}", xlno[128 * k:128 * (k + 1), :],
              c["xlnf"][b][:, :], reads=(res,), out_sem=True)
    emit_ln(p, c, x, want_f32=True, f32_sink=sink, stats_ready=io.get("stats_in"))
    xln, pb = c["xln"], c["pb"]
    for b in range(2):
        p.dma("G", f"w{b}", wt[b][:, :], wimg[b], writes=(f"wt{b}",))
    for h in range(2):
        for k in range(NKT):
            p.op("T", lambda t, h=h, k=k: t.matmul(
                pb[h][:, :], wt[0][:, 128 * k:128 * (k + 1)],
                xln[:, 1024 * k + 512 * h:1024 * k + 512 * (h + 1)],
                start=(k == 0), stop=(k == NKT - 1)),
                reads=("wt0",) + xln_reads(k), writes=(f"pb{h}",))
            p.op("T", lambda t, h=h, k=k: t.matmul(
                pb[2 + h][0:32, :], wt[1][:, 128 * k:128 * k + 32],
                xln[:, 1024 * k + 512 * h:1024 * k + 512 * (h + 1)],
                start=(k == 0), stop=(k == NKT - 1)),
                reads=("wt1",) + xln_reads(k), writes=(f"pb{2 + h}",))
        p.op("V", lambda v, h=h: v.tensor_copy(xp1[:, 512 * h:512 * (h + 1)], pb[h][:, :]),
             reads=(f"pb{h}",), writes=("xp1",))
        p.op("V", lambda v, h=h: v.tensor_copy(xp2[:, 512 * h:512 * (h + 1)], pb[2 + h][0:32, :]),
             reads=(f"pb{2 + h}",), writes=("xp2",))
    p.dma("Y", "xpo1", xpo[0:128, :], xp1[:, :], reads=("xp1",), out_sem=True)
    p.dma("Y", "xpo2", xpo[128:160, :], xp2[0:32, :], reads=("xp2",), out_sem=True)
    p.scope_close()


def build_ssm_core():
    p = Prog("ssm_core")
    io = dict(xslc=p.din("xslc", [DM, TOK], F32),
              xpf=p.din("xpf", [1280, TOK], F32),
              alog=p.din("alog", [128, 64], F32),
              dtw=p.din("dtw", [128, 512], F32),
              dtb=p.din("dtb", [128, 4], F32),
              patd=p.din("patd", [128, 16 * 128], BF16),
              patb=p.din("patb", [16, 128], BF16),
              paty=p.din("paty", [128, 16 * 128], BF16),
              yT=p.dout("yT", [DM, TOK], BF16))
    emit_ssm_core(p, io)
    return p


def emit_ssm_core(p, io):
    p.scope_open()
    xslc, xpf, alog, dtw, dtb = io["xslc"], io["xpf"], io["alog"], io["dtw"], io["dtb"]
    patd_d, patb_d, paty_d, yT = io["patd"], io["patb"], io["paty"], io["yT"]

    aimg = p.sb("aimg", [128, 64], F32)
    atmp = p.sb("atmp", [128, 64], F32)
    dtin = p.sb("dtin", [128, SEQ], BF16)
    btc = p.sb("btc", [16, 512], BF16)
    dtw_sb = p.sb("dtw_sb", [128, 512], BF16)
    dtb_sb = p.sb("dtb_sb", [128, 4], F32)
    patd_sb = p.sb("patd_sb", [128, 16 * 128], BF16)
    patb_sb = p.sb("patb_sb", [16, 128], BF16)
    paty_sb = p.sb("paty_sb", [128, 16 * 128], BF16)
    brep = p.sb("brep", [128, SEQ], BF16)
    crep = p.sb("crep", [128, SEQ], BF16)
    x_sb = p.sb("x_sb", [128, SEQ], BF16)
    dt_sb = p.sb("dt_sb", [128, SEQ], BF16)
    u_sb = p.sb("u_sb", [128, SEQ], BF16)
    sp = [[p.sb(f"sp{i}_{j}", [128, 1024], F32) for i in range(3)] for j in range(2)]
    state = p.sb("state", [128, 16], F32)
    dA = [p.sb(f"dA{b}", [128, 1024], F32) for b in range(2)]
    dBx = [p.sb(f"dBx{b}", [128, 1024], F32) for b in range(2)]
    h_sb = [p.sb(f"h_sb{b}", [128, 1024], F32) for b in range(2)]
    hc = [p.sb(f"hc{b}", [128, 1024], BF16) for b in range(2)]
    ybf = p.sb("ybf", [128, 1024], BF16)
    psA = p.psum("psA", [128, 1024], F32)
    psU = p.psum("psU", [128, 1024], F32)
    psY = p.psum("psY", [128, 1024], F32)
    psBC = p.psum("psBC", [128, 1024], F32)

    p.dma("Y", "cal", aimg[:, :], alog, writes=("alog_raw",))
    p.op("S", lambda s: s.activation(atmp[:, :], aimg[:, :], AF.Exp),
         reads=("alog_raw",), writes=("atmp",))
    p.op("V", lambda v: v.tensor_scalar(aimg[:, :], atmp[:, :], -1.0, None, ALU.mult),
         reads=("atmp",), writes=("aimg",))
    p.dma("G", "cdw", dtw_sb[:, :], dtw, writes=("dtw_sb",))
    p.dma("Y", "cdb", dtb_sb[:, :], dtb, writes=("dtb_sb",))
    p.dma("Y", "cpd", patd_sb[:, :], patd_d, writes=("patd_sb",))
    p.dma("Y", "cpb", patb_sb[:, :], patb_d, writes=("patb_sb",))
    p.dma("Y", "cpy", paty_sb[:, :], paty_d, writes=("paty_sb",))
    for i in range(NCORE):
        p.dma("G", "cdi", dtin[:, 1024 * i:1024 * (i + 1)], xpf[160 * i:160 * i + 128, :],
              writes=(f"dtin_{i}",))
    dtin_r = tuple(f"dtin_{i}" for i in range(NCORE))

    for rbase, dst, dname in ((128, brep, "brep"), (144, crep, "crep")):
        for i in range(NQT):
            rk, half = i // 2, i % 2
            p.dma("G", "cbt", btc[:, :],
                  xpf[160 * rk + rbase:160 * rk + rbase + 16, 512 * half:512 * (half + 1)],
                  writes=("btc",))
            p.op("T", lambda t: t.matmul(
                psBC[:, 0:512], patb_sb[:, :], btc[0:16, :],
                start=True, stop=True),
                reads=("btc", "patb_sb"), writes=("psBC",))
            p.op("V", lambda v, i=i, dst=dst: v.tensor_copy(dst[:, 512 * i:512 * (i + 1)], psBC[:, 0:512]),
                 reads=("psBC",), writes=(dname,))

    for dsub in range(4):
        for i in range(NCORE):
            p.dma("G", "cx", x_sb[:, 1024 * i:1024 * (i + 1)],
                  xslc[512 * i + 128 * dsub:512 * i + 128 * (dsub + 1), :],
                  writes=(f"xbuf_{i}",))
        x_r = tuple(f"xbuf_{i}" for i in range(NCORE))
        for t8 in range(8):
            j = t8 % 2
            s0, s1, s2 = sp[j]
            for hh in range(2):
                sl = slice(1024 * t8 + 512 * hh, 1024 * t8 + 512 * (hh + 1))
                p.op("T", lambda t, dsub=dsub, sl=sl, hh=hh: t.matmul(
                    psBC[:, 512 * hh:512 * (hh + 1)], dtw_sb[:, 128 * dsub:128 * (dsub + 1)],
                    dtin[:, sl], start=True, stop=True),
                    reads=dtin_r + ("dtw_sb",), writes=("psBC",))
            p.op("V", lambda v, dsub=dsub, s0=s0: v.tensor_scalar(s0[:, :], psBC[:, :],
                 dtb_sb[:, dsub:dsub + 1], None, ALU.add),
                 reads=("psBC",), sreads=("dtb_sb",), writes=(f"sp0_{j}",))
            p.op("S", lambda s, s0=s0, s1=s1: s.activation(s1[:, :], s0[:, :], AF.Abs),
                 reads=(f"sp0_{j}",), writes=(f"sp1_{j}",))
            p.op("S", lambda s, s1=s1: s.activation(s1[:, :], s1[:, :], AF.Exp, scale=-1.0),
                 reads=(f"sp1_{j}",), writes=(f"sp1_{j}",))
            p.op("V", lambda v, s1=s1: v.tensor_scalar(s1[:, :], s1[:, :], 1.0, None, ALU.add),
                 reads=(f"sp1_{j}",), writes=(f"sp1_{j}",))
            p.op("S", lambda s, s1=s1: s.activation(s1[:, :], s1[:, :], AF.Ln),
                 reads=(f"sp1_{j}",), writes=(f"sp1_{j}",))
            p.op("V", lambda v, s0=s0, s2=s2: v.tensor_scalar(s2[:, :], s0[:, :], 0.0, None, ALU.max),
                 reads=(f"sp0_{j}",), writes=(f"sp2_{j}",))
            p.op("V", lambda v, t8=t8, s1=s1, s2=s2: v.tensor_add(
                 dt_sb[:, 1024 * t8:1024 * (t8 + 1)], s1[:, :], s2[:, :]),
                 reads=(f"sp1_{j}", f"sp2_{j}"), writes=(f"dtt_{t8}",))
        dt_r = tuple(f"dtt_{t8}" for t8 in range(8))
        p.op("V", lambda v: v.tensor_mul(u_sb[:, :], dt_sb[:, :], x_sb[:, :]),
             reads=dt_r + x_r, writes=("u_sb",))
        p.op("V", lambda v: v.memset(state[:, :], 0.0), writes=("state",))
        def emit_red(grp):
            gsl = slice(128 * grp, 128 * (grp + 1))
            gb = grp % 2
            for h in range(2):
                p.op("T", lambda t, grp=grp, h=h, gsl=gsl, gb=gb: t.matmul(
                    psY[:, 512 * h:512 * (h + 1)], paty_sb[:, gsl], hc[gb][:, 512 * h:512 * (h + 1)],
                    start=(grp == 0), stop=(grp == 15)),
                    reads=(f"hc{gb}", "paty_sb"), writes=(f"psY{h}",))

        for tcB in range(8):
            csl = slice(1024 * tcB, 1024 * (tcB + 1))
            for grp in range(16):
                gsl = slice(128 * grp, 128 * (grp + 1))
                for h in range(2):
                    hsl = slice(1024 * tcB + 512 * h, 1024 * tcB + 512 * (h + 1))
                    p.op("T", lambda t, gsl=gsl, hsl=hsl, h=h: t.matmul(
                        psA[:, 512 * h:512 * (h + 1)], patd_sb[:, gsl], dt_sb[:, hsl],
                        start=True, stop=True),
                        reads=dt_r + ("patd_sb",), writes=(f"psA{h}",))
                    p.op("T", lambda t, gsl=gsl, hsl=hsl, h=h: t.matmul(
                        psU[:, 512 * h:512 * (h + 1)], patd_sb[:, gsl], u_sb[:, hsl],
                        start=True, stop=True),
                        reads=("u_sb", "patd_sb"), writes=(f"psU{h}",))
                if grp > 0:
                    emit_red(grp - 1)
                gb = grp % 2
                p.op("S", lambda s, dsub=dsub, grp=grp, gb=gb: s.activation(dA[gb][:, :], psA[:, :], AF.Exp,
                     scale=aimg[:, 16 * dsub + grp:16 * dsub + grp + 1]),
                     reads=("psA0", "psA1"), sreads=("aimg",), writes=(f"dA{gb}",))
                p.op("V", lambda v, csl=csl, gb=gb: v.tensor_mul(dBx[gb][:, :], psU[:, :], brep[:, csl]),
                     reads=("psU0", "psU1", "brep"), writes=(f"dBx{gb}",))
                p.op("V", lambda v, grp=grp, gb=gb: v.tensor_tensor_scan(h_sb[gb][:, :], dA[gb][:, :], dBx[gb][:, :],
                     state[:, grp:grp + 1], ALU.mult, ALU.add),
                     reads=(f"dA{gb}", f"dBx{gb}"), sreads=("state",), writes=(f"h_sb{gb}",))
                p.op("V", lambda v, grp=grp, gb=gb: v.tensor_copy(state[:, grp:grp + 1], h_sb[gb][:, 1023:1024]),
                     reads=(f"h_sb{gb}",), writes=("state",))
                p.op("G", lambda g, csl=csl, gb=gb: g.tensor_mul(hc[gb][:, :], h_sb[gb][:, :], crep[:, csl]),
                     reads=(f"h_sb{gb}", "crep"), writes=(f"hc{gb}",))
            emit_red(15)
            p.op("V", lambda v: v.tensor_copy(ybf[:, :], psY[:, :]),
                 reads=("psY0", "psY1"), writes=("ybf",))
            p.dma("Y", "yo", yT[512 * tcB + 128 * dsub:512 * tcB + 128 * (dsub + 1), :],
                  ybf[:, :], reads=("ybf",), out_sem=True)
    p.scope_close()


def build_ssm_post():
    p = Prog("ssm_post")
    io = dict(x=p.din("x", [DM, TOK], F32),
              xln=p.din("xln", [DM, TOK], F32),
              yT=p.din("yT", [DM, TOK], BF16),
              dimg=p.din("dimg", [128, NKT], F32),
              xo=p.dout("xo", [DM, TOK], F32))
    emit_ssm_post(p, io)
    return p


def emit_ssm_post(p, io):
    p.scope_open()
    x, xln, yT, dimg, xo = io["x"], io["xln"], io["yT"], io["dimg"], io["xo"]
    stats_out = io.get("stats_out")
    croot = io.get("croot")
    d_sb = p.sb("d_sb", [128, NKT], F32)
    nxr = 4 if stats_out else 2
    xr = [p.sb(f"xr{b}", [128, TOK], F32) for b in range(nxr)]
    xl = [p.sb(f"xl{b}", [128, TOK], F32) for b in range(2)]
    yl = [p.sb(f"yl{b}", [128, TOK], BF16) for b in range(2)]
    if stats_out:
        st = [p.psum(f"stp{i}", [1, 512], F32) for i in range(4)]
        stn = ("st0", "st1", "st2", "st3")

    def emit_stats_m(ms):
        bs = ms % nxr
        for h in range(2):
            emit_stats_tile(p, croot, st, stn, xr[bs][:, 512 * h:512 * (h + 1)],
                            f"xr{bs}", ms, h)
    p.dma("Y", "cd", d_sb[:, :], dimg, writes=("d_sb",))
    for k in range(NKT):
        b = k % 2
        bx = k % nxr
        p.dma("Y", f"x{bx}", xr[bx][:, :], x[128 * k:128 * (k + 1), :], writes=(f"xr{bx}",))
        p.dma("Y", f"l{b}", xl[b][:, :], xln[128 * k:128 * (k + 1), :], writes=(f"xl{b}",))
        p.dma("Y", f"y{b}", yl[b][:, :], yT[128 * k:128 * (k + 1), :], writes=(f"yl{b}",))
        p.op("V", lambda v, b=b, k=k: v.scalar_tensor_tensor(
            xl[b][:, :], xl[b][:, :], d_sb[:, k:k + 1], yl[b][:, :], ALU.mult, ALU.add),
            reads=(f"xl{b}", f"yl{b}"), sreads=("d_sb",), writes=(f"t{b}",))
        p.op("V", lambda v, bx=bx, b=b: v.tensor_add(xr[bx][:, :], xr[bx][:, :], xl[b][:, :]),
             reads=(f"xr{bx}", f"t{b}"), writes=(f"xr{bx}",))
        p.dma("Y", f"o{bx}", xo[128 * k:128 * (k + 1), :], xr[bx][:, :],
              reads=(f"xr{bx}",), out_sem=True)
        if stats_out and k >= 2:
            emit_stats_m(k - 2)
    if stats_out:
        emit_stats_m(NKT - 2)
        emit_stats_m(NKT - 1)
        emit_stats_fin(p, croot, st, stn)
    p.scope_close()


def host_ssm_consts(A_log, D, x_proj_w, dt_proj_w, dt_proj_b):
    xpw = np.zeros((DM, 256), dtype=np.float32)
    xpw[:, 0:160] = x_proj_w.T
    pre = dict(wimg=blocked_img(xpw),
               onescol=np.ones((128, 1), dtype=BF),
               onesrow=np.ones((1, 128), dtype=np.float32))
    patd = np.zeros((128, 16 * 128), dtype=BF)
    paty = np.zeros((128, 16 * 128), dtype=BF)
    patb = np.zeros((16, 128), dtype=BF)
    for g in range(16):
        for pp in range(128):
            patd[8 * g + pp // 16, 128 * g + pp] = 1
            paty[pp, 128 * g + 8 * g + pp // 16] = 1
    for pp in range(128):
        patb[pp % 16, pp] = 1
    cores = []
    for c in range(NCORE):
        d0 = 512 * c
        alog = np.zeros((128, 64), dtype=np.float32)
        for dsub in range(4):
            for g in range(16):
                for pp in range(128):
                    alog[pp, 16 * dsub + g] = A_log[d0 + 128 * dsub + 8 * g + pp // 16, pp % 16]
        dtw = np.zeros((128, 512), dtype=np.float32)
        for dsub in range(4):
            dtw[:, 128 * dsub:128 * (dsub + 1)] = dt_proj_w[d0 + 128 * dsub:d0 + 128 * (dsub + 1), :].T
        dtb = dt_proj_b[d0:d0 + 512].reshape(4, 128).T.astype(np.float32)
        cores.append(dict(alog=alog, dtw=dtw, dtb=np.ascontiguousarray(dtb),
                          patd=patd, patb=patb, paty=paty))
    post = dict(dimg=np.ascontiguousarray(D.reshape(NKT, 128).T).astype(np.float32))
    return pre, cores, post


def run_ssm(xT_shards, consts3):
    pre_c, cores_c, post_c = consts3
    nc1 = get_prog("ssm_pre", build_ssm_pre)
    res1 = run_bass_kernel_spmd(nc1, [dict(pre_c, x=np.ascontiguousarray(s)) for s in xT_shards],
                                core_ids=list(range(NCORE)))
    xln = [np.asarray(r["xlno"]) for r in res1.results]
    xpf = np.concatenate([np.asarray(r["xpo"]) for r in res1.results], axis=0)
    nc2 = get_prog("ssm_core", build_ssm_core)
    in2 = []
    for c in range(NCORE):
        xslc = np.concatenate([xln[i][512 * c:512 * (c + 1)] for i in range(NCORE)], axis=0)
        in2.append(dict(cores_c[c], xslc=np.ascontiguousarray(xslc), xpf=np.ascontiguousarray(xpf)))
    res2 = run_bass_kernel_spmd(nc2, in2, core_ids=list(range(NCORE)))
    yT = [np.asarray(r["yT"]) for r in res2.results]
    nc3 = get_prog("ssm_post", build_ssm_post)
    in3 = []
    for c in range(NCORE):
        ym = np.concatenate([yT[i][512 * c:512 * (c + 1)] for i in range(NCORE)], axis=0)
        in3.append(dict(post_c, x=np.ascontiguousarray(xT_shards[c]),
                        xln=np.ascontiguousarray(xln[c]), yT=np.ascontiguousarray(ym)))
    res3 = run_bass_kernel_spmd(nc3, in3, core_ids=list(range(NCORE)))
    return [r["xo"] for r in res3.results]




def build_dequant():
    p = Prog("dequant")
    io = dict(codes=p.din("codes", [DM, 1280], I32),
              scales=p.din("scales", [1, 32 * 1280], F32),
              onesrow=p.din("onesrow", [1, 128], F32),
              wcat=p.dout("wcat", [DM, 1280], BF16))
    emit_dequant(p, io)
    return p


def emit_dequant(p, io, scoped=True):
    if scoped:
        p.scope_open()
    codes, scales, ones_row_d, wcat = io["codes"], io["scales"], io["onesrow"], io["wcat"]
    ct = [p.sb(f"ct{b}", [128, 512], I32) for b in range(2)]
    cf = [p.sb(f"cf{b}", [128, 512], F32) for b in range(2)]
    wb = [p.sb(f"wb{b}", [128, 512], BF16) for b in range(2)]
    sc = p.sb("sc", [1, 1280], F32)
    onr = p.sb("onr", [1, 128], F32)
    ps = [p.psum(f"ps{i}", [128, 512], F32) for i in range(2)]
    p.dma("Y", "co", onr[:, :], ones_row_d, writes=("onr",))
    chunks = [(0, 512), (512, 512), (1024, 256)]
    i = 0
    for g in range(32):
        p.dma("Y", "cs", sc[:, :], scales[0:1, 1280 * g:1280 * (g + 1)], writes=("sc",))
        for off, w in chunks:
            b = i % 2
            p.dma("Y", f"c{b}", ct[b][:, 0:w], codes[128 * g:128 * (g + 1), off:off + w],
                  writes=(f"ct{b}",))
            p.op("V", lambda v, b=b, w=w: v.tensor_scalar(cf[b][:, 0:w], ct[b][:, 0:w],
                 8.0, 0.5, ALU.subtract, ALU.mult),
                 reads=(f"ct{b}",), writes=(f"cf{b}",))
            p.op("T", lambda t, b=b, off=off, w=w: t.matmul(ps[b][:, 0:w], onr[0:1, :],
                 sc[0:1, off:off + w], start=True, stop=True),
                 reads=("sc", "onr"), writes=(f"ps{b}",))
            p.op("V", lambda v, b=b, w=w: v.tensor_mul(wb[b][:, 0:w], cf[b][:, 0:w], ps[b][:, 0:w]),
                 reads=(f"cf{b}", f"ps{b}"), writes=(f"wb{b}",))
            p.dma("Y", f"o{b}", wcat[128 * g:128 * (g + 1), off:off + w], wb[b][:, 0:w],
                  reads=(f"wb{b}",), writes=(f"dq_st{b}",), out_sem=True)
            i += 1
    p.op("V", lambda v: v.engine_nop(), reads=("dq_st0", "dq_st1"), writes=("dq_all",))
    if scoped:
        p.scope_close()


def run_dequant(q_codes, k_codes, v_codes, o_codes, q_scales, k_scales, v_scales, o_scales):
    nc = get_prog("dequant", build_dequant)
    in_maps = []
    for c in range(NCORE):
        cc = np.concatenate([q_codes[:, 512 * c:512 * (c + 1)], o_codes[:, 512 * c:512 * (c + 1)],
                             k_codes[:, 128 * c:128 * (c + 1)], v_codes[:, 128 * c:128 * (c + 1)]],
                            axis=1).astype(np.int32)
        ss = np.concatenate([q_scales[:, 512 * c:512 * (c + 1)], o_scales[:, 512 * c:512 * (c + 1)],
                             k_scales[:, 128 * c:128 * (c + 1)], v_scales[:, 128 * c:128 * (c + 1)]],
                            axis=1).astype(np.float32)
        in_maps.append(dict(codes=np.ascontiguousarray(cc),
                            scales=np.ascontiguousarray(ss.reshape(1, -1)),
                            onesrow=np.ones((1, 128), dtype=np.float32)))
    res = run_bass_kernel_spmd(nc, in_maps, core_ids=list(range(NCORE)))
    wq = np.concatenate([np.asarray(r["wcat"])[:, 0:512] for r in res.results], axis=1)
    wo = np.concatenate([np.asarray(r["wcat"])[:, 512:1024] for r in res.results], axis=1)
    wk = np.concatenate([np.asarray(r["wcat"])[:, 1024:1152] for r in res.results], axis=1)
    wv = np.concatenate([np.asarray(r["wcat"])[:, 1152:1280] for r in res.results], axis=1)
    return wq, wk, wv, wo




def emit_attn_pre_wall(p, croot, io):
    p.scope_open()
    x, wall, qkvp = io["x"], io["wall"], io["qkvp"]
    c = stage_ln(p, croot)
    pb = c["pb"]
    slab = [p.sb(f"slab{b}", [128, 1280], BF16) for b in range(3)]
    ob = p.sb("ob", [128, 512], BF16)
    emit_ln(p, c, x, stats_ready=io.get("stats_in"))
    xln = c["xln"]
    cols = [(0, 128), (128, 128), (256, 128), (384, 128), (1024, 128), (1152, 128)]
    rowbase = [0, 128, 256, 384, 512, 640]
    for j in range(NCORE):
        for h in range(2):
            for kt in range(NKT):
                b = kt % 3
                p.dma("Y" if b == 0 else "G", f"sl{b}", slab[b][:, :],
                      wall[DM * j + 128 * kt:DM * j + 128 * (kt + 1), :],
                      reads=("wall_r",), writes=(f"slab{b}",))
                for mi, (off, wdt) in enumerate(cols):
                    p.op("T", lambda t, b=b, mi=mi, off=off, h=h, kt=kt: t.matmul(
                        pb[mi][:, :], slab[b][:, off:off + 128],
                        xln[:, 1024 * kt + 512 * h:1024 * kt + 512 * (h + 1)],
                        start=(kt == 0), stop=(kt == NKT - 1)),
                        reads=(f"slab{b}",) + xln_reads(kt), writes=(f"pb{mi}",))
            for mi in range(6):
                p.op("S", lambda s, mi=mi: s.copy(ob[:, :], pb[mi][:, :]),
                     reads=(f"pb{mi}",), writes=("ob",))
                p.dma("Y", "o0", qkvp[768 * j + rowbase[mi]:768 * j + rowbase[mi] + 128,
                      512 * h:512 * (h + 1)], ob[:, :], reads=("ob",), out_sem=True)
    p.scope_close()


def emit_attn_post_wall(p, io):
    p.scope_open()
    x, at, wall, xo = io["x"], io["at"], io["wall"], io["xo"]
    slab = [p.sb(f"slab{b}", [128, 512], BF16) for b in range(3)]
    att = p.sb("att", [128, NKT * TOK], BF16)
    xr = [p.sb(f"xr{b}", [128, TOK], F32) for b in range(2)]
    pb = [p.psum(f"pb{i}", [128, 512], F32) for i in range(8)]
    scale_tile = emit_post_rcp_prologue(p, io, att, pb)
    for k in range(NKT):
        p.dma("Y", f"a{k % 2}", att[:, 1024 * k:1024 * (k + 1)], at[128 * k:128 * (k + 1), :],
              writes=(f"att{k}",))
        scale_tile(k)
    for j in range(NCORE):
        for kt in range(NKT):
            b = kt % 3
            p.dma("Y" if b == 0 else "G", f"sl{b}", slab[b][:, :],
                  wall[DM * j + 128 * kt:DM * j + 128 * (kt + 1), 512:1024],
                  reads=("wall_r",), writes=(f"slab{b}",))
            for s in range(4):
                for h in range(2):
                    p.op("T", lambda t, b=b, s=s, h=h, kt=kt: t.matmul(
                        pb[2 * s + h][:, :], slab[b][:, 128 * s:128 * (s + 1)],
                        att[:, 1024 * kt + 512 * h:1024 * kt + 512 * (h + 1)],
                        start=(kt == 0), stop=(kt == NKT - 1)),
                        reads=(f"slab{b}", f"att{kt}"), writes=(f"pb{2 * s + h}",))
        for s in range(4):
            m = 4 * j + s
            bb = s % 2
            p.dma("Y", f"xr{bb}", xr[bb][:, :], x[128 * m:128 * (m + 1), :], writes=(f"xr{bb}",))
            for h in range(2):
                p.op("V", lambda v, bb=bb, s=s, h=h: v.tensor_add(
                    xr[bb][:, 512 * h:512 * (h + 1)], xr[bb][:, 512 * h:512 * (h + 1)],
                    pb[2 * s + h][:, :]),
                    reads=(f"xr{bb}", f"pb{2 * s + h}"), writes=(f"xr{bb}",))
            p.dma("Y", f"xo{bb}", xo[128 * m:128 * (m + 1), :], xr[bb][:, :],
                  reads=(f"xr{bb}",), out_sem=True)
    p.scope_close()


def build_fused(upto=99):
    p = Prog("mk_fused" + ("" if upto >= 99 else f"_u{upto}"))
    nc = p.nc
    oc_d = p.din("onescol", [128, 1], BF16)
    orw_d = p.din("onesrow", [1, 128], F32)
    trimask_d = p.din("trimask", [4, 128, 512], BF16)
    croot = alloc_ln_root(p)
    emit_ln_consts(p, croot, oc_d, orw_d)

    x0 = p.din("x0", [DM, TOK], F32)
    xo_ext = p.dout("xout", [DM, TOK], F32)
    xb = [nc.dram_tensor(f"xb{i}", [DM, TOK], F32) for i in range(3)]

    def attn_block(sfx, x_in, x_out, wdt):
        pre_w = p.din("wpre" + sfx, [48, 128, NKT * 128], F32 if wdt == "f32" else BF16) \
            if wdt != "dram" else None
        post_w = p.din("wpost" + sfx, [NKT, 128, NKT * 128], F32 if wdt == "f32" else BF16) \
            if wdt != "dram" else None
        return pre_w, post_w

    def run_attn_block(sfx, x_in, x_out, pre_w_ap, post_w_ap, cos_ap, sin_ap,
                       pre_stats_in=False, post_stats_out=False):
        qkvp = nc.dram_tensor("qkvp" + sfx, [6144, TOK], BF16)
        qkvm = nc.dram_tensor("qkvm" + sfx, [6144, TOK], BF16)
        aout = nc.dram_tensor("aout" + sfx, [DM, TOK], BF16)
        at = nc.dram_tensor("at" + sfx, [DM, TOK], BF16)
        dend = nc.dram_tensor("dend" + sfx, [64, 512], F32)
        dend_all = nc.dram_tensor("dend_all" + sfx, [64, 512], F32)
        dend_rcp = nc.dram_tensor("dend_rcp" + sfx, [NKT, TOK], F32)
        emit_attn_pre(p, croot, dict(x=x_in, wimg=pre_w_ap, qkvp=qkvp.ap(),
                                     stats_in=pre_stats_in))
        p.cc("AllToAll", ins=[qkvp.ap().opt()], outs=[qkvm.ap().opt()])
        p.barrier()
        emit_attn_core(p, dict(qkvm=qkvm.ap(), cosf=cos_ap, sinf=sin_ap,
                               trimask=trimask_d, onescol=oc_d, onesrow=orw_d,
                               aout=aout.ap(), dend=dend.ap(), croot=croot))
        p.cc("AllToAll", ins=[aout.ap().opt()], outs=[at.ap().opt()])
        p.cc("AllToAll", ins=[dend.ap().opt()], outs=[dend_all.ap().opt()])
        p.barrier()
        emit_attn_post(p, dict(x=x_in, at=at.ap(), wimg=post_w_ap, xo=x_out,
                               dA=dend_all.ap(), dAr=dend_rcp.ap(), onesrow=orw_d,
                               stats_out=post_stats_out, croot=croot))

    wcat = nc.dram_tensor("wcat", [DM, 1280], BF16)
    wall = nc.dram_tensor("wall", [NCORE * DM, 1280], BF16, addr_space="Shared")
    def finish_early(cur):
        p.scope_open()
        fb = [p.sb(f"fb{b}", [128, TOK], F32) for b in range(2)]
        for k in range(NKT):
            b = k % 2
            p.dma("Y", f"fl{b}", fb[b][:, :], cur[128 * k:128 * (k + 1), :], writes=(f"fb{b}",))
            p.dma("Y", f"fs{b}", xo_ext[128 * k:128 * (k + 1), :], fb[b][:, :],
                  reads=(f"fb{b}",), out_sem=True)
        p.scope_close()
        return p

    if upto <= -3:
        return finish_early(x0)
    emit_dequant(p, dict(codes=p.din("codes", [DM, 1280], I32),
                         scales=p.din("scales", [1, 32 * 1280], F32),
                         onesrow=orw_d, wcat=wcat.ap()), scoped=True)
    if upto <= -2:
        return finish_early(x0)
    p.cc("AllGather", ins=[wcat.ap().opt()], outs=[wall.ap().opt()], reads=("dq_all",),
         writes=("wall_r",))
    if upto <= -1:
        return finish_early(x0)
    wall_ap = wall.ap()



    if upto <= 0:
        return finish_early(x0)
    run_attn_block("a0", x0, xb[0].ap(),
                   p.din("wpre_a0", [48, 128, NKT * 128], F32),
                   p.din("wpost_a0", [NKT, 128, NKT * 128], F32),
                   p.din("cos0", [128, SEQ], F32), p.din("sin0", [128, SEQ], F32),
                   pre_stats_in=False, post_stats_out=True)
    if upto <= 1:
        return finish_early(xb[0].ap())
    iof0 = ffn_io(p, "_f0", with_x=False)
    iof0["x"] = xb[0].ap(); iof0["xo"] = xb[1].ap()
    iof0["stats_in"] = True; iof0["stats_out"] = True
    emit_ffn(p, croot, iof0, "l0")
    if upto <= 2:
        return finish_early(xb[1].ap())
    xlno = nc.dram_tensor("xlno", [DM, TOK], F32)
    xpo = nc.dram_tensor("xpo", [160, TOK], F32)
    xslc = nc.dram_tensor("xslc", [DM, TOK], F32)
    xpf = nc.dram_tensor("xpf", [1280, TOK], F32, addr_space="Shared")
    yTb = nc.dram_tensor("yTb", [DM, TOK], BF16)
    ymb = nc.dram_tensor("ymb", [DM, TOK], BF16)
    emit_ssm_pre(p, croot, dict(x=xb[1].ap(), wimg=p.din("wimg_sp", [2, 128, NKT * 128], F32),
                                xlno=xlno.ap(), xpo=xpo.ap(), stats_in=True))
    p.cc("AllToAll", ins=[xlno.ap().opt()], outs=[xslc.ap().opt()])
    p.cc("AllGather", ins=[xpo.ap().opt()], outs=[xpf.ap().opt()])
    p.barrier()
    emit_ssm_core(p, dict(xslc=xslc.ap(), xpf=xpf.ap(),
                          alog=p.din("alog", [128, 64], F32),
                          dtw=p.din("dtw", [128, 512], F32),
                          dtb=p.din("dtb", [128, 4], F32),
                          patd=p.din("patd", [128, 16 * 128], BF16),
                          patb=p.din("patb", [16, 128], BF16),
                          paty=p.din("paty", [128, 16 * 128], BF16),
                          yT=yTb.ap()))
    p.cc("AllToAll", ins=[yTb.ap().opt()], outs=[ymb.ap().opt()])
    p.barrier()
    emit_ssm_post(p, dict(x=xb[1].ap(), xln=xlno.ap(), yT=ymb.ap(),
                          dimg=p.din("dimg", [128, NKT], F32), xo=xb[2].ap(),
                          stats_out=True, croot=croot))
    if upto <= 3:
        return finish_early(xb[2].ap())
    iof0b = dict(iof0)
    iof0b["x"] = xb[2].ap(); iof0b["xo"] = xb[0].ap()
    iof0b["stats_in"] = True; iof0b["stats_out"] = True
    emit_ffn(p, croot, iof0b, "l0")
    if upto <= 4:
        return finish_early(xb[0].ap())
    run_attn_block("a1", xb[0].ap(), xb[1].ap(),
                   p.din("wpre_a1", [48, 128, NKT * 128], F32),
                   p.din("wpost_a1", [NKT, 128, NKT * 128], F32),
                   p.din("cos1", [128, SEQ], F32), p.din("sin1", [128, SEQ], F32),
                   pre_stats_in=True, post_stats_out=True)
    if upto <= 5:
        return finish_early(xb[1].ap())
    iof1 = ffn_io(p, "_f1", with_x=False)
    iof1["x"] = xb[1].ap(); iof1["xo"] = xb[2].ap()
    iof1["stats_in"] = True; iof1["stats_out"] = True
    emit_ffn(p, croot, iof1, "l1")
    if upto <= 6:
        return finish_early(xb[2].ap())
    qkvp_ab = nc.dram_tensor("qkvp_ab", [6144, TOK], BF16)
    qkvm_ab = nc.dram_tensor("qkvm_ab", [6144, TOK], BF16)
    aout_ab = nc.dram_tensor("aout_ab", [DM, TOK], BF16)
    at_ab = nc.dram_tensor("at_ab", [DM, TOK], BF16)
    dend_ab = nc.dram_tensor("dend_ab", [64, 512], F32)
    dend_all_ab = nc.dram_tensor("dend_all_ab", [64, 512], F32)
    dend_rcp_ab = nc.dram_tensor("dend_rcp_ab", [NKT, TOK], F32)
    emit_attn_pre_wall(p, croot, dict(x=xb[2].ap(), wall=wall_ap, qkvp=qkvp_ab.ap(),
                                      stats_in=True))
    p.cc("AllToAll", ins=[qkvp_ab.ap().opt()], outs=[qkvm_ab.ap().opt()])
    p.barrier()
    emit_attn_core(p, dict(qkvm=qkvm_ab.ap(), cosf=p.din("cos1b", [128, SEQ], F32),
                           sinf=p.din("sin1b", [128, SEQ], F32),
                           trimask=trimask_d, onescol=oc_d, onesrow=orw_d,
                           aout=aout_ab.ap(), dend=dend_ab.ap(), croot=croot))
    p.cc("AllToAll", ins=[aout_ab.ap().opt()], outs=[at_ab.ap().opt()])
    p.cc("AllToAll", ins=[dend_ab.ap().opt()], outs=[dend_all_ab.ap().opt()])
    p.barrier()
    emit_attn_post_wall(p, dict(x=xb[2].ap(), at=at_ab.ap(), wall=wall_ap, xo=xb[0].ap(),
                                dA=dend_all_ab.ap(), dAr=dend_rcp_ab.ap(), onesrow=orw_d))
    if upto <= 7:
        return finish_early(xb[0].ap())
    iof1b = dict(iof1)
    iof1b["x"] = xb[0].ap(); iof1b["xo"] = xo_ext
    iof1b["stats_in"] = False; iof1b["stats_out"] = False
    emit_ffn(p, croot, iof1b, "l1")
    return p


def kernel(**inputs):
    x = np.asarray(inputs["L0_attn_a_x"])[0]
    c_a0 = host_attn_consts("l0", inputs["L0_attn_a_Wq"], inputs["L0_attn_a_Wk"],
                            inputs["L0_attn_a_Wv"], inputs["L0_attn_a_Wo"])
    c_a1 = host_attn_consts("l1", inputs["L1_attn_a_Wq"], inputs["L1_attn_a_Wk"],
                            inputs["L1_attn_a_Wv"], inputs["L1_attn_a_Wo"])
    c_f0 = host_ffn_consts("l0", inputs["L0_ffn_W_base"], inputs["L0_ffn_b_base"],
                           inputs["L0_ffn_W_router"], None,
                           inputs["L0_ffn_lora_A"], inputs["L0_ffn_lora_B"])
    c_f1 = host_ffn_consts("l1", inputs["L1_ffn_W_base"], inputs["L1_ffn_b_base"],
                           inputs["L1_ffn_W_router"], inputs["L1_ffn_b_router"],
                           inputs["L1_ffn_lora_A"], inputs["L1_ffn_lora_B"])
    c_s_pre, c_s_cores, c_s_post = host_ssm_consts(
        inputs["L0_attn_b_A_log"], inputs["L0_attn_b_D"], inputs["L0_attn_b_x_proj_w"],
        inputs["L0_attn_b_dt_proj_w"], inputs["L0_attn_b_dt_proj_b"])
    cos1b, sin1b = host_rope_tables("l1")

    nc = get_prog("fused", lambda: build_fused())

    def ffn_maps(c, sfx):
        return {"wimg" + sfx: c["wimg"], "bbi" + sfx: c["bbi"], "wr" + sfx: c["wr"],
                "br" + sfx: c["br"], "aimg" + sfx: c["aimg"], "bstk" + sfx: c["bstk"],
                "patw" + sfx: c["patw"], "idf" + sfx: c["idf"]}

    in_maps = []
    for c in range(NCORE):
        cc_codes = np.concatenate(
            [inputs["L1_attn_b_q_codes"][:, 512 * c:512 * (c + 1)],
             inputs["L1_attn_b_o_codes"][:, 512 * c:512 * (c + 1)],
             inputs["L1_attn_b_k_codes"][:, 128 * c:128 * (c + 1)],
             inputs["L1_attn_b_v_codes"][:, 128 * c:128 * (c + 1)]], axis=1).astype(np.int32)
        cc_scales = np.concatenate(
            [inputs["L1_attn_b_q_scales"][:, 512 * c:512 * (c + 1)],
             inputs["L1_attn_b_o_scales"][:, 512 * c:512 * (c + 1)],
             inputs["L1_attn_b_k_scales"][:, 128 * c:128 * (c + 1)],
             inputs["L1_attn_b_v_scales"][:, 128 * c:128 * (c + 1)]], axis=1).astype(np.float32)
        m = dict(
            onescol=np.ones((128, 1), dtype=BF),
            onesrow=np.ones((1, 128), dtype=np.float32),
            trimask=host_trimask(),
            x0=np.ascontiguousarray(x[1024 * c:1024 * (c + 1)].T),
            codes=np.ascontiguousarray(cc_codes),
            scales=np.ascontiguousarray(cc_scales.reshape(1, -1)),
            wpre_a0=c_a0[0]["wimg"], wpost_a0=c_a0[2]["wimg"],
            cos0=c_a0[1]["cosf"], sin0=c_a0[1]["sinf"],
            wpre_a1=c_a1[0]["wimg"], wpost_a1=c_a1[2]["wimg"],
            cos1=c_a1[1]["cosf"], sin1=c_a1[1]["sinf"],
            cos1b=cos1b, sin1b=sin1b,
            wimg_sp=c_s_pre["wimg"], dimg=c_s_post["dimg"],
            **c_s_cores[c],
            **ffn_maps(c_f0, "_f0"), **ffn_maps(c_f1, "_f1"),
        )
        in_maps.append(m)
    res = run_bass_kernel_spmd(nc, in_maps, core_ids=list(range(NCORE)))
    out = np.concatenate([np.asarray(r["xout"]).T for r in res.results], axis=0)
    return out[None].astype(np.float32)
